# version 03
# baseline (speedup 1.0000x reference)
import math
import numpy as np
import ml_dtypes
import concourse.bass as bass
import concourse.mybir as mybir
from concourse.bass_utils import run_bass_kernel_spmd
from concourse.library_overlay import lower_extended_insts
from concourse import library_config
from contextlib import ExitStack

F32 = mybir.dt.float32
BF16 = mybir.dt.bfloat16
I16 = mybir.dt.int16
I32 = mybir.dt.int32
AF = mybir.ActivationFunctionType
ALU = mybir.AluOpType
AX = mybir.AxisListType

NCORE = 8
TOK = 1024
D = 2048
NCH = 16
NTT = 2
TWO_PI = 2.0 * math.pi
EPS = 1e-5

DEBUG_TAPS = []


def build_nc(dev=False):
    devk = dict(kind="ExternalOutput") if dev else {}
    nc = bass.Bass("TRN2", target_bir_lowering=False, num_devices=NCORE,
                   detect_race_conditions=False)

    xfm = nc.dram_tensor("xfm", [NCH, 128, TOK + 1], F32, kind="ExternalInput").ap()
    wact_re = nc.dram_tensor("wact_re", [8, 128, 32 * 128], F32, kind="ExternalInput").ap()
    wact_im = nc.dram_tensor("wact_im", [8, 128, 32 * 128], F32, kind="ExternalInput").ap()
    wrec = nc.dram_tensor("wrec", [16, 128, 32 * 128], F32, kind="ExternalInput").ap()
    wval = nc.dram_tensor("wval", [16, 128, 16 * 128], F32, kind="ExternalInput").ap()
    wq = nc.dram_tensor("wq", [1, 128, 16 * 128], F32, kind="ExternalInput").ap()
    keys_in = nc.dram_tensor("keys_in", [2, 64, 64], F32, kind="ExternalInput").ap()
    de_pk = nc.dram_tensor("de_pk", [32, 128, 16 * 128], F32, kind="ExternalInput").ap()
    ue_pk = nc.dram_tensor("ue_pk", [16, 128, 32 * 128], F32, kind="ExternalInput").ap()
    gate_pk = nc.dram_tensor("gate_pk", [64, 128, 16 * 128], F32, kind="ExternalInput").ap()
    up_pk = nc.dram_tensor("up_pk", [64, 128, 16 * 128], F32, kind="ExternalInput").ap()
    down_pk = nc.dram_tensor("down_pk", [16, 128, 64 * 128], F32, kind="ExternalInput").ap()
    identb = nc.dram_tensor("identb", [128, 128], BF16, kind="ExternalInput").ap()
    onesb = nc.dram_tensor("onesb", [128, 1], F32, kind="ExternalInput").ap()
    iota64 = nc.dram_tensor("iota64", [128, 64], F32, kind="ExternalInput").ap()
    mask_max = nc.dram_tensor("mask_max", [128, 8], F32, kind="ExternalInput").ap()
    mask_add = nc.dram_tensor("mask_add", [128, 8], F32, kind="ExternalInput").ap()
    mask_e3 = nc.dram_tensor("mask_e3", [64, 32], F32, kind="ExternalInput").ap()

    out_o = nc.dram_tensor("o", [D, TOK], F32, kind="ExternalOutput").ap()

    r4_spill = nc.dram_tensor("r4_spill", [NCH * 128, TOK], F32, **devk).ap()
    phi_spill = nc.dram_tensor("phi_spill", [8 * 128, TOK], F32, **devk).ap()
    hfm_spill = nc.dram_tensor("hfm_spill", [NCH * 128, TOK], F32, **devk).ap()
    kv_dump = nc.dram_tensor("kv_dump", [16 * 128, TOK], BF16, **devk).ap() if dev else None
    q_dump = nc.dram_tensor("q_dump", [64, 2 * TOK], BF16, **devk).ap() if dev else None
    qb_dump = nc.dram_tensor("qb_dump", [64, 2 * TOK], BF16, **devk).ap() if dev else None
    sm_dump = nc.dram_tensor("sm_dump", [128, 64], BF16, **devk).ap() if dev else None
    iq_dump = nc.dram_tensor("iq_dump", [128, 256], BF16, **devk).ap() if dev else None
    exp_dump = nc.dram_tensor("exp_dump", [NCH * 128, TOK], F32, **devk).ap() if dev else None
    tk_dump = nc.dram_tensor("tk_dump", [128, 320], F32, **devk).ap() if dev else None
    macc_dump = nc.dram_tensor("macc_dump", [NCH * 128, TOK], F32, **devk).ap() if dev else None
    ln1_rows = nc.dram_tensor("ln1_rows", [2, TOK], F32).ap()
    ln2_rows = nc.dram_tensor("ln2_rows", [2, TOK], F32).ap()
    cc1_in = nc.dram_tensor("cc1_in", [128, 16], F32).ap()
    cc1_dump = nc.dram_tensor("cc1_dump", [128, 16], F32, **devk).ap() if dev else None
    cc1_out = nc.dram_tensor("cc1_out", [NCORE * 128, 16], F32, addr_space="Shared").ap()
    cc2_in = nc.dram_tensor("cc2_in", [128, 8], F32).ap()
    cc2_out = nc.dram_tensor("cc2_out", [NCORE * 128, 8], F32, addr_space="Shared").ap()
    cc3_in = nc.dram_tensor("cc3_in", [64, 2 * TOK], BF16).ap()
    cc3_out = nc.dram_tensor("cc3_out", [NCORE * 64, 2 * TOK], BF16, addr_space="Shared").ap()

    ctx = ExitStack()
    E = ctx.enter_context

    s_iota = E(nc.sbuf_tensor("s_iota", [128, 64], F32))
    s_mmax = E(nc.sbuf_tensor("s_mmax", [128, 8], F32))
    s_madd = E(nc.sbuf_tensor("s_madd", [128, 8], F32))
    s_me3 = E(nc.sbuf_tensor("s_me3", [64, 32], F32))
    s_idb = E(nc.sbuf_tensor("s_idb", [128, 128], BF16))
    s_ones = E(nc.sbuf_tensor("s_ones", [128, 1], F32))
    s_keys = E(nc.sbuf_tensor("s_keys", [64, 2, 64], BF16))
    wsl = [E(nc.sbuf_tensor(f"wsl{i}", [128, 32 * 128], BF16)) for i in range(2)]
    s_agg = E(nc.sbuf_tensor("s_agg", [128, 16], F32))
    s_ag1 = E(nc.sbuf_tensor("s_ag1", [128, 8, 16], F32))
    s_agg2 = E(nc.sbuf_tensor("s_agg2", [128, 8], F32))
    s_ag2 = E(nc.sbuf_tensor("s_ag2", [128, 8, 8], F32))
    s_prev1 = E(nc.sbuf_tensor("s_prev1", [128, 16], F32))
    s_prev2 = E(nc.sbuf_tensor("s_prev2", [128, 8], F32))
    s_tmp38 = E(nc.sbuf_tensor("s_tmp38", [128, 16, 8], F32))
    s_srow = E(nc.sbuf_tensor("s_srow", [1, 2 * TOK], F32))
    s_mu = E(nc.sbuf_tensor("s_mu", [1, TOK], F32))
    s_inv = E(nc.sbuf_tensor("s_inv", [1, TOK], F32))
    s_sm = E(nc.sbuf_tensor("s_sm", [128, 8, 8], BF16))
    s_iq4 = E(nc.sbuf_tensor("s_iq4", [128, 8, 4, 8], I16))
    s_neg1 = E(nc.sbuf_tensor("s_neg1", [128, 8], F32))
    s_zero = E(nc.sbuf_tensor("s_zero", [128, TOK], F32))
    s_topk = E(nc.sbuf_tensor("s_topk", [128, 64], F32))
    s_topk2 = E(nc.sbuf_tensor("s_topk2", [128, 64], F32))
    s_t8 = E(nc.sbuf_tensor("s_t8", [128, 4, 8], F32))
    s_i8 = E(nc.sbuf_tensor("s_i8", [128, 4, 8], F32))
    s_eq = E(nc.sbuf_tensor("s_eq", [128, 8, 64], F32))
    s_a64 = E(nc.sbuf_tensor("s_a64", [128, 2, 64], F32))
    s_sm1 = E(nc.sbuf_tensor("s_sm1", [128, 8], F32))
    s_ssum = E(nc.sbuf_tensor("s_ssum", [128, 1], F32))
    cm1f = E(nc.sbuf_tensor("cm1f", [128, 1], F32))
    nc.const_aps.register(F32, -1.0, cm1f[:])
    chpi = E(nc.sbuf_tensor("chpi", [128, 1], F32))
    nc.const_aps.register(F32, math.pi / 2, chpi[:])
    chf = E(nc.sbuf_tensor("chf", [128, 1], F32))
    nc.const_aps.register(F32, 0.5, chf[:])

    ps = [E(nc.psum_tensor(f"ps{i}", [128, 512], F32)) for i in range(8)]

    s_cn = E(nc.semaphore("s_cn"))
    s_sq = E(nc.semaphore("s_sq"))
    s_x = E(nc.semaphore("s_x"))
    s_w = [E(nc.semaphore(f"s_w{i}")) for i in range(2)]
    s_pp = [E(nc.semaphore(f"s_pp{i}")) for i in range(2)]
    s_pr = [E(nc.semaphore(f"s_pr{i}")) for i in range(2)]
    s_sp = [E(nc.semaphore(f"s_sp{i}")) for i in range(2)]
    s_xr = [E(nc.semaphore(f"s_xr{i}")) for i in range(2)]
    s_sh = E(nc.semaphore("s_sh"))
    s_out = E(nc.semaphore("s_out"))
    s_vb = E(nc.semaphore("s_vb"))
    _vbn = [0]

    def VB(ins, v):
        ins.then_inc(s_vb, 1)
        _vbn[0] += 1
        v.wait_ge(s_vb, _vbn[0])
        return ins

    lcc = E(nc.semaphore("lcc"))
    tte = E(nc.semaphore("tte"))
    vve = E(nc.semaphore("vve"))
    sse = E(nc.semaphore("sse"))
    gge = E(nc.semaphore("gge"))

    block = E(nc.Block())

    g_rsig = nc.sbuf_tensor("rsig", [128, 2, TOK], F32); rsig = g_rsig.__enter__()
    g_kvbf = nc.sbuf_tensor("kvbf", [128, NCH, TOK], BF16, side="right"); kvbf = g_kvbf.__enter__()
    g_xbf = nc.sbuf_tensor("xbf", [128, NCH, TOK + 1], BF16); xbf = g_xbf.__enter__()
    g_mag2 = nc.sbuf_tensor("mag2", [128, 8, TOK], F32); mag2 = g_mag2.__enter__()
    g_trans = nc.sbuf_tensor("trans", [128, 8, TOK], F32); trans = g_trans.__enter__()
    g_rtmp = nc.sbuf_tensor("rtmp", [128, TOK], F32); rtmp = g_rtmp.__enter__()
    g_rst = nc.sbuf_tensor("rst", [128, 2, TOK], F32); rst = g_rst.__enter__()
    g_phb = nc.sbuf_tensor("phb", [128, 2, TOK], F32); phb = g_phb.__enter__()

    @block.sync
    def _(sync):
        sync.dma_start(out=s_iota[:], in_=iota64).then_inc(s_cn, 16)
        sync.dma_start(out=s_mmax[:], in_=mask_max).then_inc(s_cn, 16)
        sync.dma_start(out=s_madd[:], in_=mask_add).then_inc(s_cn, 16)
        sync.dma_start(out=s_me3[:], in_=mask_e3).then_inc(s_cn, 16)
        sync.dma_start(out=s_idb[:], in_=identb).then_inc(s_cn, 16)
        sync.dma_start(out=s_ones[:], in_=onesb).then_inc(s_cn, 16)
    CN_ALL = 6 * 16

    @block.gpsimd
    def _(g):
        g.load_library(library_config.local_scatter)
        for k in range(NCH):
            g.dma_start(out=xbf[:, k, :], in_=xfm[k]).then_inc(s_x, 16)
        g.dma_start(out=s_keys[0:64, 0, :], in_=keys_in[0]).then_inc(s_x, 16)
        g.dma_start(out=s_keys[0:64, 1, :], in_=keys_in[1]).then_inc(s_x, 16)
    X_ALL = 18 * 16

    @block.vector
    def _(v):
        v.memset(cm1f[:], -1.0)
        v.memset(chpi[:], math.pi / 2)
        v.memset(chf[:], 0.5)
        v.memset(s_neg1[:], -1.0)
        v.memset(s_zero[:], 0.0).then_inc(vve, 1)
    VVE0 = 1

    TT1, SSE1, VVE1 = 0, 0, VVE0

    def re_t(j): return trans[:, (j % 2) * 4 + 0, :]
    def im_t(j): return trans[:, (j % 2) * 4 + 1, :]
    def sq1_t(j): return trans[:, (j % 2) * 4 + 2, :]
    def sq2_t(j): return trans[:, (j % 2) * 4 + 3, :]

    @block.gpsimd
    def _(g):
        for p in range(16):
            if p >= 2:
                g.wait_ge(tte, TT1 + p - 1)
            srcp = (wact_re if p % 2 == 0 else wact_im)[p // 2]
            g.dma_start(out=wsl[p % 2][:], in_=srcp).then_inc(s_w[p % 2], 16)

    @block.tensor
    def _(t):
        t.wait_ge(s_x, X_ALL)
        for p in range(16):
            t.wait_ge(s_w[p % 2], (p // 2 + 1) * 16)
            if p >= 2:
                jp, pl = (p - 2) // 2, (p - 2) % 2
                t.wait_ge(sse, SSE1 + 4 * jp + 1 + pl)
            last = None
            for k in range(32):
                ch = k if k < 16 else k - 16
                off = 0 if k < 16 else 1
                for tt in range(NTT):
                    last = t.matmul(ps[(p % 2) * 2 + tt][:],
                                    wsl[p % 2][:, k * 128:(k + 1) * 128],
                                    xbf[:, ch, off + tt * 512: off + tt * 512 + 512],
                                    start=(k == 0), stop=(k == 31))
            last.then_inc(tte, 1)
    TT_P1 = TT1 + 16

    @block.scalar
    def _(s):
        for j in range(8):
            if j >= 2:
                s.wait_ge(vve, VVE1 + 3 * (j - 2) + 3)
            s.wait_ge(tte, TT1 + 2 * j + 1)
            s.activation(re_t(j)[:, 0:512], ps[(2 * j % 2) * 2 + 0][:], AF.Sigmoid)
            s.activation(re_t(j)[:, 512:1024], ps[1][:], AF.Sigmoid).then_inc(sse, 1)
            s.wait_ge(tte, TT1 + 2 * j + 2)
            s.activation(im_t(j)[:, 0:512], ps[2][:], AF.Sigmoid)
            s.activation(im_t(j)[:, 512:1024], ps[3][:], AF.Sigmoid).then_inc(sse, 1)
            s.wait_ge(vve, VVE1 + 3 * j + 1)
            s.activation(sq1_t(j)[:], re_t(j)[:], AF.Square)
            s.activation(sq2_t(j)[:], im_t(j)[:], AF.Square).then_inc(sse, 1)
            s.wait_ge(vve, VVE1 + 3 * j + 2)
            if j >= 2:
                s.wait_ge(s_pp[j % 2], (j // 2) * 16)
            s.activation(phb[:, j % 2, :], sq2_t(j)[:], AF.Arctan).then_inc(sse, 1)
    SSE_P1 = SSE1 + 32

    @block.vector
    def _(v):
        for j in range(8):
            v.wait_ge(sse, SSE1 + 4 * j + 2)
            v.tensor_scalar_max(re_t(j)[:], re_t(j)[:], 0.5)
            v.tensor_scalar_max(im_t(j)[:], im_t(j)[:], 0.5).then_inc(vve, 1)
            v.wait_ge(sse, SSE1 + 4 * j + 3)
            v.tensor_add(mag2[:, j, :], sq1_t(j)[:], sq2_t(j)[:])
            v.reciprocal(sq1_t(j)[:], re_t(j)[:])
            v.tensor_mul(sq1_t(j)[:], im_t(j)[:], sq1_t(j)[:])
            v.reciprocal(sq2_t(j)[:], im_t(j)[:])
            v.tensor_mul(sq2_t(j)[:], re_t(j)[:], sq2_t(j)[:])
            v.tensor_tensor(sq2_t(j)[:], sq1_t(j)[:], sq2_t(j)[:], ALU.min)
            v.tensor_reduce(s_agg[:, j:j + 1], mag2[:, j, :], AX.X, ALU.max).then_inc(vve, 1)
            v.wait_ge(sse, SSE1 + 4 * j + 4)
            v.tensor_scalar(sq2_t(j)[:], sq1_t(j)[:], 1.0, None, ALU.is_le)
            v.tensor_scalar(sq2_t(j)[:], sq2_t(j)[:], 2.0, -1.0, ALU.mult, ALU.add)
            v.tensor_mul(phb[:, j % 2, :], phb[:, j % 2, :], sq2_t(j)[:])
            v.tensor_scalar(sq1_t(j)[:], sq2_t(j)[:], -math.pi / 4, math.pi / 4, ALU.mult, ALU.add)
            v.tensor_add(phb[:, j % 2, :], phb[:, j % 2, :], sq1_t(j)[:])
            v.tensor_reduce(s_agg[:, 8 + j:8 + j + 1], phb[:, j % 2, :], AX.X, ALU.add).then_inc(vve, 1)
    VVE_P1 = VVE1 + 24

    @block.sync
    def _(sync):
        for j in range(8):
            sync.wait_ge(vve, VVE1 + 3 * j + 3)
            sync.dma_start(out=phi_spill[j * 128:(j + 1) * 128, :], in_=phb[:, j % 2, :]).then_inc(s_pp[j % 2], 16)

    @block.sync
    def _(sync):
        sync.wait_ge(vve, VVE_P1)
        sync.dma_start(out=cc1_in, in_=s_agg[:]).then_inc(s_sq, 16)
    SQ_CC1IN = 16

    @block.gpsimd
    def _(g):
        g.wait_ge(s_sq, SQ_CC1IN)
        g.collective_compute("AllGather", ALU.bypass, replica_groups=[list(range(NCORE))],
                             ins=[cc1_in.opt()], outs=[cc1_out.opt()]).then_inc(lcc, 1)
    LCC1 = 1

    @block.sync
    def _(sync):
        sync.wait_ge(lcc, LCC1)
        for r in range(8):
            sync.dma_start(out=s_ag1[:, r, :], in_=cc1_out[r * 128:(r + 1) * 128, :]).then_inc(s_sq, 16)
    SQ_AG1 = SQ_CC1IN + 8 * 16

    TT2, SSE2, VVE2 = TT_P1, SSE_P1, VVE_P1

    @block.gpsimd
    def _(g):
        for p in range(16):
            g.wait_ge(tte, TT2 - 2 + p + 1)
            g.dma_start(out=wsl[p % 2][:], in_=wrec[p]).then_inc(s_w[p % 2], 16)

    @block.tensor
    def _(t):
        for p in range(16):
            t.wait_ge(s_w[p % 2], (8 + p // 2 + 1) * 16)
            if p >= 2:
                t.wait_ge(sse, SSE2 + (p - 2) + 1)
            last = None
            for k in range(32):
                ch = k if k < 16 else k - 16
                off = 0 if k < 16 else 1
                for tt in range(NTT):
                    last = t.matmul(ps[(p % 2) * 2 + tt][:],
                                    wsl[p % 2][:, k * 128:(k + 1) * 128],
                                    xbf[:, ch, off + tt * 512: off + tt * 512 + 512],
                                    start=(k == 0), stop=(k == 31))
            last.then_inc(tte, 1)
    TT_P2 = TT2 + 16

    @block.scalar
    def _(s):
        for p in range(16):
            s.wait_ge(tte, TT2 + p + 1)
            if p >= 2:
                s.wait_ge(vve, VVE2 + (p - 2) + 1)
            s.activation(rsig[:, p % 2, 0:512], ps[(p % 2) * 2 + 0][:], AF.Sigmoid)
            s.activation(rsig[:, p % 2, 512:1024], ps[(p % 2) * 2 + 1][:], AF.Sigmoid).then_inc(sse, 1)
    SSE_P2R = SSE2 + 16

    @block.vector
    def _(v):
        for p in range(16):
            v.wait_ge(sse, SSE2 + p + 1)
            if p >= 2:
                v.wait_ge(s_sp[p % 2], (p // 2) * 16)
            v.tensor_scalar_max(rsig[:, p % 2, :], rsig[:, p % 2, :], 0.5)
            v.tensor_mul(rtmp[:], rsig[:, p % 2, :], rsig[:, p % 2, :])
            v.tensor_mul(rst[:, p % 2, :], rtmp[:], rtmp[:]).then_inc(vve, 1)
    VVE_P2R = VVE2 + 16

    @block.sync
    def _(sync):
        for p in range(16):
            sync.wait_ge(vve, VVE2 + p + 1)
            sync.dma_start(out=r4_spill[p * 128:(p + 1) * 128, :], in_=rst[:, p % 2, :]).then_inc(s_sp[p % 2], 16)

    VVE_RSB = VVE_P2R
    SSE_RSB = SSE_P2R

    @block.vector
    def _(v):
        v.wait_ge(s_sq, SQ_AG1)
        v.wait_ge(s_cn, CN_ALL)
        a3 = s_ag1[:].transpose([0, 2, 1])
        m3m = s_mmax[:].unsqueeze(1).broadcast_to([128, 16, 8])
        m3a = s_madd[:].unsqueeze(1).broadcast_to([128, 16, 8])
        VB(v.tensor_tensor(s_tmp38[:], a3, m3m, ALU.add), v)
        VB(v.tensor_reduce(s_prev1[:, 0:8].unsqueeze(2), s_tmp38[:, 0:8, :], AX.X, ALU.max), v)
        VB(v.tensor_tensor(s_tmp38[:], a3, m3a, ALU.mult), v)
        VB(v.tensor_reduce(s_prev1[:, 8:16].unsqueeze(2), s_tmp38[:, 8:16, :], AX.X, ALU.add), v)
        v.engine_nop().then_inc(vve, 1)
        for j in range(8):
            if j >= 2:
                v.wait_ge(sse, SSE_RSB + (j - 2) + 1)
            v.tensor_tensor_scan(trans[:, (j % 2) * 3 + 0, :], mag2[:, j, :], mag2[:, j, :],
                                 s_prev1[:, j:j + 1], ALU.max, ALU.max).then_inc(vve, 1)
            v.wait_ge(sse, SSE_RSB + j + 1)
            v.tensor_sub(mag2[:, j, :], trans[:, (j % 2) * 3 + 1, :], trans[:, (j % 2) * 3 + 2, :])
            v.tensor_reduce(s_agg2[:, j:j + 1], mag2[:, j, :], AX.X, ALU.add).then_inc(vve, 1)
    VVE_RS = VVE_RSB + 1 + 16

    @block.scalar
    def _(s):
        for j in range(8):
            s.wait_ge(vve, VVE_RSB + 1 + 2 * j + 1)
            s.activation(trans[:, (j % 2) * 3 + 1, :], mag2[:, j, :], AF.Ln)
            s.activation(trans[:, (j % 2) * 3 + 2, :], trans[:, (j % 2) * 3 + 0, :], AF.Ln).then_inc(sse, 1)
    SSE_RS = SSE_RSB + 8

    @block.sync
    def _(sync):
        sync.wait_ge(vve, VVE_RS)
        sync.dma_start(out=cc2_in, in_=s_agg2[:]).then_inc(s_sq, 16)
    SQ_CC2IN = SQ_AG1 + 16

    @block.gpsimd
    def _(g):
        g.wait_ge(s_sq, SQ_CC2IN)
        g.collective_compute("AllGather", ALU.bypass, replica_groups=[list(range(NCORE))],
                             ins=[cc2_in.opt()], outs=[cc2_out.opt()]).then_inc(lcc, 1)
    LCC2 = 2

    @block.sync
    def _(sync):
        sync.wait_ge(lcc, LCC2)
        for r in range(8):
            sync.dma_start(out=s_ag2[:, r, :], in_=cc2_out[r * 128:(r + 1) * 128, :]).then_inc(s_sq, 16)
        for j in range(8):
            if j >= 2:
                sync.wait_ge(vve, VVE_RS + 1 + 3 * (j - 2) + 3)
            sync.wait_ge(s_pp[j % 2], (j // 2 + 1) * 16)
            sync.dma_start(out=phb[:, j % 2, :], in_=phi_spill[j * 128:(j + 1) * 128, :]).then_inc(s_pr[j % 2], 16)
    SQ_AG2 = SQ_CC2IN + 8 * 16

    VVE_FB = VVE_RS
    SSE_FB = SSE_RS

    @block.vector
    def _(v):
        v.wait_ge(s_sq, SQ_AG2)
        a2 = s_ag2[:].transpose([0, 2, 1])
        m3a = s_madd[:].unsqueeze(1).broadcast_to([128, 8, 8])
        VB(v.tensor_tensor(s_tmp38[:, 0:8, :], a2, m3a, ALU.mult), v)
        VB(v.tensor_reduce(s_prev2[:].unsqueeze(2), s_tmp38[:, 0:8, :], AX.X, ALU.add), v)
        v.engine_nop().then_inc(vve, 1)
        for j in range(8):
            if j >= 1:
                v.wait_ge(sse, SSE_FB + 2 * (j - 1) + 1)
            v.wait_ge(s_pr[j % 2], (j // 2 + 1) * 16)
            v.tensor_tensor_scan(trans[:, 0, :], mag2[:, j, :], s_zero[:],
                                 s_prev2[:, j:j + 1], ALU.add, ALU.add)
            v.tensor_tensor_scan(trans[:, 1, :], phb[:, j % 2, :], s_zero[:],
                                 s_prev1[:, 8 + j:8 + j + 1], ALU.add, ALU.add).then_inc(vve, 1)
            v.tensor_scalar_mul(trans[:, 2, :], trans[:, 1, :], 1.0 / TWO_PI)
            v.tensor_copy(trans[:, 3, :].bitcast(I32), trans[:, 2, :])
            v.tensor_copy(trans[:, 2, :], trans[:, 3, :].bitcast(I32))
            v.scalar_tensor_tensor(trans[:, 3, :], trans[:, 2, :], -TWO_PI, trans[:, 1, :], ALU.mult, ALU.add)
            v.tensor_scalar(trans[:, 2, :], trans[:, 3, :], math.pi, None, ALU.is_gt)
            v.scalar_tensor_tensor(trans[:, 3, :], trans[:, 2, :], -TWO_PI, trans[:, 3, :], ALU.mult, ALU.add)
            v.tensor_scalar(trans[:, 2, :], trans[:, 3, :], -math.pi, None, ALU.is_lt)
            v.scalar_tensor_tensor(trans[:, 3, :], trans[:, 2, :], TWO_PI, trans[:, 3, :], ALU.mult, ALU.add)
            v.tensor_scalar_max(trans[:, 3, :], trans[:, 3, :], -math.pi)
            v.tensor_scalar_min(trans[:, 3, :], trans[:, 3, :], math.pi)
            v.tensor_scalar(trans[:, 4, :], trans[:, 1, :], 1.0 / TWO_PI, 0.25, ALU.mult, ALU.add)
            v.tensor_copy(trans[:, 5, :].bitcast(I32), trans[:, 4, :])
            v.tensor_copy(trans[:, 4, :], trans[:, 5, :].bitcast(I32))
            v.scalar_tensor_tensor(trans[:, 5, :], trans[:, 4, :], -TWO_PI, trans[:, 1, :], ALU.mult, ALU.add)
            v.tensor_scalar(trans[:, 4, :], trans[:, 5, :], 0.5 * math.pi, None, ALU.is_gt)
            v.scalar_tensor_tensor(trans[:, 5, :], trans[:, 4, :], -TWO_PI, trans[:, 5, :], ALU.mult, ALU.add)
            v.tensor_scalar(trans[:, 4, :], trans[:, 5, :], -1.5 * math.pi, None, ALU.is_lt)
            v.scalar_tensor_tensor(trans[:, 5, :], trans[:, 4, :], TWO_PI, trans[:, 5, :], ALU.mult, ALU.add)
            v.tensor_scalar_max(trans[:, 5, :], trans[:, 5, :], -1.5 * math.pi)
            v.tensor_scalar_min(trans[:, 5, :], trans[:, 5, :], 0.5 * math.pi).then_inc(vve, 1)
            v.wait_ge(sse, SSE_FB + 2 * j + 2)
            v.tensor_mul(kvbf[:, j, :], trans[:, 6, :], trans[:, 2, :])
            v.tensor_mul(kvbf[:, 8 + j, :], trans[:, 6, :], trans[:, 7, :]).then_inc(vve, 1)
    VVE_KV = VVE_FB + 1 + 24

    @block.scalar
    def _(s):
        for j in range(8):
            s.wait_ge(vve, VVE_FB + 1 + 3 * j + 1)
            s.activation(trans[:, 6, :], trans[:, 0, :], AF.Exp, scale=0.5).then_inc(sse, 1)
            s.wait_ge(vve, VVE_FB + 1 + 3 * j + 2)
            if j >= 1:
                s.wait_ge(vve, VVE_FB + 1 + 3 * (j - 1) + 3)
            s.activation(trans[:, 7, :], trans[:, 3, :], AF.Sin)
            s.activation(trans[:, 2, :], trans[:, 5, :], AF.Sin, bias=math.pi / 2).then_inc(sse, 1)
    SSE_KV = SSE_FB + 16

    if dev:
        s_dev = nc.semaphore("s_dev").__enter__()

        @block.sync
        def _(sync):
            sync.dma_start(out=cc1_dump, in_=s_agg[:]).then_inc(s_dev, 16)
            for j in range(8):
                sync.wait_ge(vve, VVE_FB + 1 + 3 * j + 3)
                sync.dma_start(out=kv_dump[j * 128:(j + 1) * 128, :], in_=kvbf[:, j, :]).then_inc(s_dev, 16)
                sync.dma_start(out=kv_dump[(8 + j) * 128:(9 + j) * 128, :], in_=kvbf[:, 8 + j, :]).then_inc(s_dev, 16)
            sync.wait_ge(s_dev, 272)

    for _g in (g_phb, g_rst, g_rtmp, g_trans, g_mag2, g_xbf):
        _g.__exit__(None, None, None)
    bcst = nc.sbuf_tensor("bcast", [128, 2, TOK], F32).__enter__()
    x2 = nc.sbuf_tensor("x2", [128, 2, TOK], F32).__enter__()
    r42 = nc.sbuf_tensor("r42", [128, 2, TOK], F32).__enter__()
    g_hbf = nc.sbuf_tensor("hbf", [128, NCH, TOK], BF16); hbf = g_hbf.__enter__()
    g_hpre = nc.sbuf_tensor("hpre", [128, NCH, TOK], F32); hpre = g_hpre.__enter__()

    TT3, SSE3, VVE3 = TT_P2, SSE_KV, VVE_KV

    def att_done(p):
        return TT3 + p + 1 if p < 2 else TT3 + 2 * p - 1

    def stats1_done(sp):
        return TT3 + 2 * sp + 4 if sp <= 13 else TT3 + 31 + (sp - 14)

    @block.gpsimd
    def _(g):
        for p in range(16):
            if p >= 2:
                g.wait_ge(tte, att_done(p - 2))
            g.dma_start(out=wsl[p % 2][:, 0:16 * 128], in_=wval[p]).then_inc(s_w[p % 2], 16)

    @block.sync
    def _(sync):
        for p in range(16):
            if p >= 2:
                sync.wait_ge(vve, VVE3 + (p - 2) + 1)
            sync.dma_start(out=x2[:, p % 2, :], in_=xfm[p][:, 1:TOK + 1]).then_inc(s_xr[p % 2], 16)
            sync.wait_ge(s_sp[p % 2], (p // 2 + 1) * 16)
            sync.dma_start(out=r42[:, p % 2, :], in_=r4_spill[p * 128:(p + 1) * 128, :]).then_inc(s_xr[p % 2], 16)

    @block.tensor
    def _(t):
        t.wait_ge(vve, VVE_KV)
        t.wait_ge(sse, SSE_P2R)
        t.wait_ge(s_cn, CN_ALL)

        def emit_stats1(sp):
            t.wait_ge(sse, SSE3 + sp + 1)
            lastm = None
            for tt in range(NTT):
                t.matmul(ps[4 + tt][0:1, :], s_ones[:], hpre[:, sp, tt * 512:(tt + 1) * 512],
                         start=(sp == 0), stop=(sp == 15))
                lastm = t.matmul(ps[6 + tt][0:1, :], s_ones[:], rsig[:, sp % 2, tt * 512:(tt + 1) * 512],
                                 start=(sp == 0), stop=(sp == 15))
            lastm.then_inc(tte, 1)

        for p in range(16):
            t.wait_ge(s_w[p % 2], (16 + p // 2 + 1) * 16)
            if p >= 2:
                t.wait_ge(vve, VVE3 + (p - 2) + 1)
            last = None
            for k in range(16):
                for tt in range(NTT):
                    last = t.matmul(ps[(p % 2) * 2 + tt][:],
                                    wsl[p % 2][:, k * 128:(k + 1) * 128],
                                    kvbf[:, k, tt * 512:(tt + 1) * 512],
                                    start=(k == 0), stop=(k == 15))
            last.then_inc(tte, 1)
            if p >= 2:
                emit_stats1(p - 2)
        emit_stats1(14)
        emit_stats1(15)
    TT_P3 = TT3 + 32

    @block.vector
    def _(v):
        for p in range(16):
            if p == 0:
                v.wait_ge(s_sp[0], 8 * 16)
                v.wait_ge(s_sp[1], 8 * 16)
            v.wait_ge(tte, att_done(p))
            v.wait_ge(s_xr[p % 2], (p // 2 + 1) * 32)
            for tt in range(NTT):
                v.tensor_mul(hpre[:, p, tt * 512:(tt + 1) * 512], ps[(p % 2) * 2 + tt][:],
                             r42[:, p % 2, tt * 512:(tt + 1) * 512])
            if p >= 2:
                v.wait_ge(sse, SSE3 + (p - 2) + 1)
            v.tensor_add(hpre[:, p, :], hpre[:, p, :], x2[:, p % 2, :]).then_inc(vve, 1)
    VVE_HP = VVE3 + 16

    @block.scalar
    def _(s):
        for p in range(16):
            s.wait_ge(vve, VVE3 + p + 1)
            if p >= 2:
                s.wait_ge(tte, stats1_done(p - 2))
            s.activation(rsig[:, p % 2, :], hpre[:, p, :], AF.Square).then_inc(sse, 1)
    SSE_HSQ = SSE3 + 16

    @block.vector
    def _(v):
        v.wait_ge(tte, TT_P3)
        v.tensor_copy(s_srow[:, 0:512], ps[4][0:1, :])
        v.tensor_copy(s_srow[:, 512:1024], ps[5][0:1, :])
        v.tensor_copy(s_srow[:, 1024:1536], ps[6][0:1, :])
        v.tensor_copy(s_srow[:, 1536:2048], ps[7][0:1, :])
        v.tensor_scalar_mul(s_mu[:], s_srow[:, 0:TOK], 1.0 / D)
        v.tensor_scalar_mul(s_inv[:], s_srow[:, TOK:2 * TOK], 1.0 / D)
        v.tensor_mul(s_srow[:, 0:TOK], s_mu[:], s_mu[:])
        v.tensor_sub(s_inv[:], s_inv[:], s_srow[:, 0:TOK])
        v.tensor_scalar_add(s_inv[:], s_inv[:], EPS).then_inc(vve, 1)
    VVE_VAR1 = VVE_HP + 1

    @block.scalar
    def _(s):
        s.wait_ge(vve, VVE_VAR1)
        s.activation(s_srow[:, 0:TOK], s_inv[:], AF.Sqrt).then_inc(sse, 1)
    SSE_SQ1 = SSE_HSQ + 1

    @block.vector
    def _(v):
        v.wait_ge(sse, SSE_SQ1)
        v.reciprocal(s_inv[:], s_srow[:, 0:TOK]).then_inc(vve, 1)
    VVE_INV1 = VVE_VAR1 + 1

    @block.sync
    def _(sync):
        sync.wait_ge(vve, VVE_INV1)
        sync.dma_start(out=ln1_rows[0:1, :], in_=s_mu[:]).then_inc(s_sq, 16)
        sync.dma_start(out=ln1_rows[1:2, :], in_=s_inv[:]).then_inc(s_sq, 16)
        sync.wait_ge(s_sq, SQ_AG2 + 32)
        sync.dma_start(out=bcst[:, 0, :], in_=ln1_rows[0:1, :].broadcast_to([128, TOK])).then_inc(s_sq, 16)
        sync.dma_start(out=bcst[:, 1, :], in_=ln1_rows[1:2, :].broadcast_to([128, TOK])).then_inc(s_sq, 16)
    SQ_BC1 = SQ_AG2 + 4 * 16


    @block.vector
    def _(v):
        v.wait_ge(s_sq, SQ_BC1)
        for p in range(16):
            v.tensor_sub(hpre[:, p, :], hpre[:, p, :], bcst[:, 0, :])
            v.tensor_mul(hpre[:, p, :], hpre[:, p, :], bcst[:, 1, :])
            v.tensor_copy(hbf[:, p, :], hpre[:, p, :]).then_inc(vve, 1)
    VVE_H1 = VVE_INV1 + 16

    @block.sync
    def _(sync):
        for p in range(16):
            sync.wait_ge(vve, VVE_INV1 + p + 1)
            sync.dma_start(out=hfm_spill[p * 128:(p + 1) * 128, :], in_=hpre[:, p, :]).then_inc(s_sh, 16)
    SH_ALL = 16 * 16

    g_hpre.__exit__(None, None, None)
    g_q64 = nc.sbuf_tensor("q64", [64, 2 * TOK], F32); q64 = g_q64.__enter__()
    g_q64b = nc.sbuf_tensor("q64b", [64, 2 * TOK], BF16); q64b = g_q64b.__enter__()
    g_qg = nc.sbuf_tensor("qg", [64, 8, 2 * TOK], BF16); qg = g_qg.__enter__()
    g_qb = nc.sbuf_tensor("qb", [64, 2, TOK], BF16); qb = g_qb.__enter__()

    @block.gpsimd
    def _(g):
        g.wait_ge(tte, att_done(15))
        g.dma_start(out=wsl[0][:, 0:16 * 128], in_=wq[0]).then_inc(s_w[0], 16)

    @block.tensor
    def _(t):
        t.wait_ge(s_w[0], 25 * 16)
        t.wait_ge(vve, VVE_H1)
        last = None
        for k in range(16):
            for tt in range(NTT):
                t.matmul(ps[0 + tt][0:64, :], wsl[0][:, k * 128:k * 128 + 64],
                         hbf[:, k, tt * 512:(tt + 1) * 512], start=(k == 0), stop=(k == 15))
                last = t.matmul(ps[2 + tt][0:64, :], wsl[0][:, k * 128 + 64:(k + 1) * 128],
                                hbf[:, k, tt * 512:(tt + 1) * 512], start=(k == 0), stop=(k == 15))
        last.then_inc(tte, 1)
    TT_PROJ = TT_P3 + 1

    @block.vector
    def _(v):
        v.wait_ge(tte, TT_PROJ)
        for tt in range(NTT):
            dst = q64[0:64, tt * 1024:(tt + 1) * 1024].rearrange("p (s two) -> p s two", two=2)
            v.tensor_copy(dst[:, :, 0:1], ps[0 + tt][0:64, :].unsqueeze(2))
            v.tensor_copy(dst[:, :, 1:2], ps[2 + tt][0:64, :].unsqueeze(2))
        v.tensor_copy(q64b[0:64, :], q64[0:64, :]).then_inc(vve, 1)
    VVE_QSL = VVE_H1 + 1

    @block.sync
    def _(sync):
        sync.wait_ge(vve, VVE_QSL)
        sync.dma_start(out=cc3_in, in_=q64b[0:64, :]).then_inc(s_sq, 16)
    SQ_CC3IN = SQ_BC1 + 16

    @block.gpsimd
    def _(g):
        g.wait_ge(s_sq, SQ_CC3IN)
        g.collective_compute("AllGather", ALU.bypass, replica_groups=[list(range(NCORE))],
                             ins=[cc3_in.opt()], outs=[cc3_out.opt()]).then_inc(lcc, 1)
    LCC3 = 3

    @block.sync
    def _(sync):
        sync.wait_ge(lcc, LCC3)
        for r in range(8):
            sync.dma_start(out=qg[0:64, r, :], in_=cc3_out[r * 64:(r + 1) * 64, :]).then_inc(s_sq, 16)
    SQ_CCQ = SQ_CC3IN + 8 * 16

    @block.vector
    def _(v):
        v.wait_ge(s_sq, SQ_CCQ)
        for tq in range(2):
            first = True
            for r in range(tq * 4, tq * 4 + 4):
                for h in range(2):
                    mcol = s_me3[0:64, tq * 16 + r * 2 + h: tq * 16 + r * 2 + h + 1]
                    srcq = qg[0:64, r, h * 1024:(h + 1) * 1024]
                    if first:
                        v.tensor_scalar(qb[0:64, tq, :], srcq, mcol, None, ALU.mult)
                        first = False
                    else:
                        v.scalar_tensor_tensor(qb[0:64, tq, :], srcq, mcol, qb[0:64, tq, :], ALU.mult, ALU.add)
        v.engine_nop().then_inc(vve, 1)
    VVE_QB = VVE_QSL + 1

    TT5, VVE5, SSE5 = TT_PROJ, VVE_QB, SSE_SQ1

    @block.tensor
    def _(t):
        t.wait_ge(vve, VVE_QB)
        t.wait_ge(s_x, X_ALL)
        for w in range(8):
            if w >= 2:
                t.wait_ge(vve, VVE5 + 2 * (w - 2) + 1)
            b = 4 + (w % 2) * 2
            t.matmul(ps[b][:, 0:64], qb[0:64, 0, w * 128:(w + 1) * 128],
                     s_keys[0:64, 0, :], start=True, stop=True)
            t.matmul(ps[b + 1][:, 0:64], qb[0:64, 1, w * 128:(w + 1) * 128],
                     s_keys[0:64, 1, :], start=True, stop=True).then_inc(tte, 1)
    TT_ROUT = TT5 + 8

    @block.vector
    def _(v):
        v.wait_ge(s_cn, CN_ALL)
        for w in range(8):
            if dev and w == 1:
                v.wait_ge(s_dev, 272 + 5 * 16 + 3 * 16)
            v.wait_ge(tte, TT5 + w + 1)
            b = 4 + (w % 2) * 2
            VB(v.tensor_copy(s_topk[:], ps[b][:, 0:64]), v)
            VB(v.tensor_copy(s_topk2[:], ps[b + 1][:, 0:64]), v)
            VB(v.max(s_t8[:, 0, :], s_topk[:]), v)
            VB(v.max(s_t8[:, 1, :], s_topk2[:]), v)
            for half, srct in ((0, s_topk), (1, s_topk2)):
                VB(v.tensor_tensor(s_eq[:], srct[:].unsqueeze(1).broadcast_to([128, 8, 64]),
                                   s_t8[:, half, :].unsqueeze(2).broadcast_to([128, 8, 64]), ALU.is_equal), v)
                VB(v.tensor_tensor(s_eq[:], s_eq[:], s_iota[:].unsqueeze(1).broadcast_to([128, 8, 64]), ALU.mult), v)
                VB(v.tensor_reduce(s_i8[:, half, :].unsqueeze(2), s_eq[:], AX.X, ALU.add), v)
            VB(v.tensor_tensor(s_a64[:, 0, :].rearrange("p (a b) -> p a b", a=8),
                               s_t8[:, 0, :].unsqueeze(2).broadcast_to([128, 8, 8]),
                               s_t8[:, 1, :].unsqueeze(1).broadcast_to([128, 8, 8]), ALU.add), v)
            VB(v.tensor_scalar_mul(s_i8[:, 3, :], s_i8[:, 0, :], 64.0), v)
            VB(v.tensor_tensor(s_a64[:, 1, :].rearrange("p (a b) -> p a b", a=8),
                               s_i8[:, 3, :].unsqueeze(2).broadcast_to([128, 8, 8]),
                               s_i8[:, 1, :].unsqueeze(1).broadcast_to([128, 8, 8]), ALU.add), v)
            VB(v.max(s_t8[:, 2, :], s_a64[:, 0, :]), v)
            VB(v.tensor_tensor(s_eq[:], s_a64[:, 0, :].unsqueeze(1).broadcast_to([128, 8, 64]),
                               s_t8[:, 2, :].unsqueeze(2).broadcast_to([128, 8, 64]), ALU.is_equal), v)
            VB(v.tensor_tensor(s_eq[:], s_eq[:], s_a64[:, 1, :].unsqueeze(1).broadcast_to([128, 8, 64]), ALU.mult), v)
            VB(v.tensor_reduce(s_i8[:, 2, :].unsqueeze(2), s_eq[:], AX.X, ALU.add), v)
            VB(v.tensor_scalar(s_sm1[:], s_t8[:, 2, :], s_t8[:, 2, 0:1], None, ALU.subtract), v)
            v.engine_nop().then_inc(vve, 1)
            v.wait_ge(sse, SSE5 + w + 1)
            VB(v.reciprocal(s_ssum[:], s_ssum[:]), v)
            VB(v.tensor_scalar(s_sm1[:], s_sm1[:], s_ssum[:, 0:1], None, ALU.mult), v)
            VB(v.tensor_copy(s_sm[:, w, :], s_sm1[:]), v)
            for qq in range(4):
                lo = 1024.0 * qq
                VB(v.tensor_scalar(s_t8[:, 3, :], s_i8[:, 2, :], lo, None, ALU.is_ge), v)
                VB(v.tensor_scalar(s_i8[:, 3, :], s_i8[:, 2, :], lo + 1024.0, None, ALU.is_lt), v)
                VB(v.tensor_mul(s_t8[:, 3, :], s_t8[:, 3, :], s_i8[:, 3, :]), v)
                VB(v.tensor_scalar(s_i8[:, 3, :], s_i8[:, 2, :], lo - 1.0, None, ALU.subtract), v)
                VB(v.tensor_mul(s_i8[:, 3, :], s_i8[:, 3, :], s_t8[:, 3, :]), v)
                VB(v.tensor_scalar(s_i8[:, 3, :], s_i8[:, 3, :], -1.0, None, ALU.add), v)
                VB(v.tensor_copy(s_iq4[:, w, qq, :], s_i8[:, 3, :]), v)
            v.engine_nop().then_inc(vve, 1)
    VVE_TOPK = VVE5 + 16

    @block.scalar
    def _(s):
        for w in range(8):
            s.wait_ge(vve, VVE5 + 2 * w + 1)
            s.activation(s_sm1[:], s_sm1[:], AF.Exp, accum_out=s_ssum[:]).then_inc(sse, 1)
    SSE_TOPK = SSE5 + 8

    if dev:
        @block.sync
        def _(sync):
            sync.wait_ge(vve, VVE5 + 2)
            sync.dma_start(out=tk_dump[:, 0:64], in_=s_topk[:]).then_inc(s_dev, 16)
            sync.dma_start(out=tk_dump[:, 64:128], in_=s_topk2[:]).then_inc(s_dev, 16)
            sync.dma_start(out=tk_dump[:, 128:160], in_=s_t8[:].rearrange("p a b -> p (a b)")).then_inc(s_dev, 16)
            sync.dma_start(out=tk_dump[:, 160:192], in_=s_i8[:].rearrange("p a b -> p (a b)")).then_inc(s_dev, 16)
            sync.dma_start(out=tk_dump[:, 192:320], in_=s_a64[:].rearrange("p a b -> p (a b)")).then_inc(s_dev, 16)

    if dev:
        @block.sync
        def _(sync):
            sync.wait_ge(vve, VVE_QSL)
            sync.dma_start(out=q_dump, in_=q64b[0:64, :]).then_inc(s_dev, 16)
            sync.wait_ge(vve, VVE_QB)
            sync.dma_start(out=qb_dump[0:64, 0:TOK], in_=qb[0:64, 0, :]).then_inc(s_dev, 16)
            sync.dma_start(out=qb_dump[0:64, TOK:2 * TOK], in_=qb[0:64, 1, :]).then_inc(s_dev, 16)
            sync.wait_ge(vve, VVE_TOPK)
            sync.dma_start(out=sm_dump, in_=s_sm[:].rearrange("p a b -> p (a b)")).then_inc(s_dev, 16)
            sync.dma_start(out=iq_dump, in_=s_iq4[:].rearrange("p a b c -> p (a b c)").bitcast(BF16)).then_inc(s_dev, 16)

    g_qb.__exit__(None, None, None)
    g_qg.__exit__(None, None, None)
    g_q64b.__exit__(None, None, None)
    g_q64.__exit__(None, None, None)
    g_kvbf.__exit__(None, None, None)
    g_mlp = nc.sbuf_tensor("mlpacc", [128, NCH, TOK], F32); mlpacc = g_mlp.__enter__()
    g_sd = nc.sbuf_tensor("sd", [128, TOK], BF16); sd = g_sd.__enter__()
    g_s0t = nc.sbuf_tensor("s0t", [128, 8, TOK], BF16); s0t = g_s0t.__enter__()
    g_s0h = nc.sbuf_tensor("s0h", [128, 1024], BF16); s0h = g_s0h.__enter__()

    TTA, VVEA, SSEA = TT_ROUT, VVE_TOPK, SSE_TOPK

    def tr_done(q, w, e8):
        return TTA + q * 88 + w * 8 + e8 + 1

    def tr_w_done(q, w):
        return TTA + q * 88 + (w + 1) * 8

    def d_done(q, e):
        return TTA + q * 88 + 64 + e + 1

    def sup_done(q, m):
        return TTA + q * 88 + 72 + m + 1

    TT_P5A = TTA + 4 * 88

    def gu_done(r, j):
        return TT_P5A + r * 32 + j + 1

    def dn_done(r, m):
        return TT_P5A + r * 32 + 16 + m + 1

    TT_P5B = TT_P5A + 128

    def tr_evac(q, w, e8):
        return VVEA + q * 88 + w * 8 + e8 + 1

    def mul_done(q, e):
        return VVEA + q * 88 + 64 + e + 1

    def supv(q, m):
        return VVEA + q * 88 + 72 + m + 1

    VVE_P5A = VVEA + 4 * 88

    def inv_in(r, j):
        return VVE_P5A + r * 32 + j + 1

    def inv_dn(r, m):
        return VVE_P5A + r * 32 + 16 + m + 1

    VVE_P5B = VVE_P5A + 128

    def sil(q, e):
        return SSEA + q * 8 + e + 1

    SSE_P5A = SSEA + 32

    def silb(r, j):
        return SSE_P5A + r * 16 + j + 1

    SSE_P5B = SSE_P5A + 64

    def scat_g(q, w):
        return q * 8 + w + 1

    panel_tte = []
    for q in range(4):
        for e in range(8):
            panel_tte.append(d_done(q, e))
        for m in range(16):
            panel_tte.append(sup_done(q, m))
    for r in range(4):
        for j in range(16):
            panel_tte.append(gu_done(r, j))
        for m in range(16):
            panel_tte.append(dn_done(r, m))

    panel_wv = []
    _wc = [25, 24]
    _kinds = []
    for q in range(4):
        for e in range(8):
            _kinds.append(1)
        for m in range(16):
            _kinds.append(1)
    for r in range(4):
        for j in range(16):
            _kinds.append(2)
        for m in range(16):
            _kinds.append(1)
    for _pi, _k in enumerate(_kinds):
        _wc[_pi % 2] += _k
        panel_wv.append(_wc[_pi % 2] * 16)

    @block.gpsimd
    def _(g):
        pi = 0
        for q in range(4):
            for w in range(8):
                if q == 0 and w == 0:
                    g.wait_ge(vve, VVE_TOPK)
                else:
                    pq, pw = (q, w - 1) if w > 0 else (q - 1, 7)
                    g.wait_ge(tte, tr_w_done(pq, pw))
                g.local_scatter(s0h[:], s_sm[:, w, :], s_iq4[:, w, q, :],
                                channels=128, num_elems=1024, num_idxs=8).then_inc(gge, 1)
            for e in range(8):
                g.wait_ge(tte, panel_tte[pi - 2] if pi >= 2 else TT_PROJ)
                g.dma_start(out=wsl[pi % 2][:, 0:16 * 128], in_=de_pk[q * 8 + e]).then_inc(s_w[pi % 2], 16)
                pi += 1
            for m in range(16):
                g.wait_ge(tte, panel_tte[pi - 2] if pi >= 2 else TT_PROJ)
                g.dma_start(out=wsl[pi % 2][:, 0:8 * 128], in_=ue_pk[m][:, q * 1024:(q + 1) * 1024]).then_inc(s_w[pi % 2], 16)
                pi += 1
        for r in range(4):
            for j in range(16):
                g.wait_ge(tte, panel_tte[pi - 2])
                g.dma_start(out=wsl[pi % 2][:, 0:16 * 128], in_=gate_pk[16 * r + j]).then_inc(s_w[pi % 2], 16)
                g.dma_start(out=wsl[pi % 2][:, 16 * 128:32 * 128], in_=up_pk[16 * r + j]).then_inc(s_w[pi % 2], 16)
                pi += 1
            for m in range(16):
                g.wait_ge(tte, panel_tte[pi - 2])
                g.dma_start(out=wsl[pi % 2][:, 0:16 * 128], in_=down_pk[m][:, r * 2048:(r + 1) * 2048]).then_inc(s_w[pi % 2], 16)
                pi += 1

    @block.tensor
    def _(t):
        pi = 0
        t.wait_ge(s_cn, CN_ALL)
        for q in range(4):
            for w in range(8):
                t.wait_ge(gge, scat_g(q, w))
                for e8 in range(8):
                    gt = q * 64 + w * 8 + e8
                    bank = ps[6 + gt % 2]
                    if gt < 2:
                        t.wait_ge(vve, VVE_TOPK)
                    elif w == 0 and e8 < 2:
                        t.wait_ge(vve, supv(q - 1, 15))
                    else:
                        pg = gt - 2
                        t.wait_ge(vve, tr_evac(pg // 64, (pg % 64) // 8, pg % 8))
                    t.matmul(bank[:, 0:128], s0h[:, e8 * 128:(e8 + 1) * 128], s_idb[:],
                             start=True, stop=True).then_inc(tte, 1)
            for e in range(8):
                ge = q * 8 + e
                t.wait_ge(s_w[pi % 2], panel_wv[pi]); pi += 1
                if ge >= 2:
                    t.wait_ge(sse, sil((ge - 2) // 8, (ge - 2) % 8))
                else:
                    t.wait_ge(vve, VVE_TOPK)
                last = None
                for k in range(16):
                    for tt in range(NTT):
                        last = t.matmul(ps[(ge % 2) * 2 + tt][:],
                                        wsl[(pi - 1) % 2][:, k * 128:(k + 1) * 128],
                                        hbf[:, k, tt * 512:(tt + 1) * 512],
                                        start=(k == 0), stop=(k == 15))
                last.then_inc(tte, 1)
            for m in range(16):
                gm = q * 16 + m
                t.wait_ge(s_w[pi % 2], panel_wv[pi]); pi += 1
                t.wait_ge(vve, mul_done(q, 7))
                if gm >= 2:
                    t.wait_ge(vve, supv((gm - 2) // 16, (gm - 2) % 16))
                else:
                    t.wait_ge(vve, VVE_TOPK)
                last = None
                for e in range(8):
                    for tt in range(NTT):
                        last = t.matmul(ps[4 + (gm % 2) * 2 + tt][:],
                                        wsl[(pi - 1) % 2][:, e * 128:(e + 1) * 128],
                                        s0t[:, e, tt * 512:(tt + 1) * 512],
                                        start=(e == 0), stop=(e == 7))
                last.then_inc(tte, 1)
        assert pi == 96

    @block.scalar
    def _(s):
        for q in range(4):
            for e in range(8):
                ge = q * 8 + e
                s.wait_ge(tte, d_done(q, e))
                if ge >= 1:
                    pg = ge - 1
                    s.wait_ge(vve, mul_done(pg // 8, pg % 8))
                s.activation(sd[:, 0:512], ps[(ge % 2) * 2 + 0][:], AF.Silu)
                s.activation(sd[:, 512:1024], ps[(ge % 2) * 2 + 1][:], AF.Silu).then_inc(sse, 1)
        for r in range(4):
            for j in range(16):
                jj = 16 * r + j
                s.wait_ge(tte, gu_done(r, j))
                if jj >= 1:
                    s.wait_ge(vve, inv_in((jj - 1) // 16, (jj - 1) % 16))
                else:
                    s.wait_ge(vve, mul_done(3, 7))
                gub = (jj % 2) * 4
                s.activation(sd[:, 0:512], ps[gub + 0][:], AF.Silu)
                s.activation(sd[:, 512:1024], ps[gub + 1][:], AF.Silu).then_inc(sse, 1)

    @block.vector
    def _(v):
        for q in range(4):
            for w in range(8):
                for e8 in range(8):
                    if q > 0 and w == 0 and e8 == 0:
                        v.wait_ge(tte, sup_done(q - 1, 15))
                    v.wait_ge(tte, tr_done(q, w, e8))
                    bank = ps[6 + (q * 64 + w * 8 + e8) % 2]
                    v.tensor_copy(s0t[:, e8, w * 128:(w + 1) * 128], bank[:, 0:128]).then_inc(vve, 1)
            for e in range(8):
                v.wait_ge(sse, sil(q, e))
                v.tensor_tensor(s0t[:, e, :], s0t[:, e, :], sd[:], ALU.mult).then_inc(vve, 1)
            for m in range(16):
                gm = q * 16 + m
                v.wait_ge(tte, sup_done(q, m))
                if q == 0:
                    if m == 0:
                        v.wait_ge(s_sh, SH_ALL)
                    for tt in range(NTT):
                        v.tensor_copy(mlpacc[:, m, tt * 512:(tt + 1) * 512], ps[4 + (gm % 2) * 2 + tt][:])
                else:
                    for tt in range(NTT):
                        v.tensor_add(mlpacc[:, m, tt * 512:(tt + 1) * 512],
                                     mlpacc[:, m, tt * 512:(tt + 1) * 512], ps[4 + (gm % 2) * 2 + tt][:])
                v.engine_nop().then_inc(vve, 1)


    g_s0h.__exit__(None, None, None)
    g_s0t.__exit__(None, None, None)
    g_inter = nc.sbuf_tensor("inter", [128, 16, TOK], BF16); inter = g_inter.__enter__()

    @block.tensor
    def _(t):
        pi = 96
        for r in range(4):
            for j in range(16):
                jj = 16 * r + j
                t.wait_ge(s_w[pi % 2], panel_wv[pi]); pi += 1
                if jj >= 2:
                    t.wait_ge(sse, silb((jj - 2) // 16, (jj - 2) % 16))
                    t.wait_ge(vve, inv_in((jj - 2) // 16, (jj - 2) % 16))
                elif jj == 0:
                    t.wait_ge(sse, SSE_P5A)
                else:
                    t.wait_ge(vve, VVE_P5A)
                gub = (jj % 2) * 4
                last = None
                for k in range(16):
                    for tt in range(NTT):
                        t.matmul(ps[gub + tt][:], wsl[(pi - 1) % 2][:, k * 128:(k + 1) * 128],
                                 hbf[:, k, tt * 512:(tt + 1) * 512], start=(k == 0), stop=(k == 15))
                        last = t.matmul(ps[gub + 2 + tt][:],
                                        wsl[(pi - 1) % 2][:, 16 * 128 + k * 128:16 * 128 + (k + 1) * 128],
                                        hbf[:, k, tt * 512:(tt + 1) * 512], start=(k == 0), stop=(k == 15))
                last.then_inc(tte, 1)
            for m in range(16):
                t.wait_ge(s_w[pi % 2], panel_wv[pi]); pi += 1
                t.wait_ge(vve, inv_in(r, 15))
                if m >= 2:
                    t.wait_ge(vve, inv_dn(r, m - 2))
                else:
                    t.wait_ge(vve, inv_in(r, 14 + m))
                last = None
                for k in range(16):
                    for tt in range(NTT):
                        last = t.matmul(ps[4 + (m % 2) * 2 + tt][:],
                                        wsl[(pi - 1) % 2][:, k * 128:(k + 1) * 128],
                                        inter[:, k, tt * 512:(tt + 1) * 512],
                                        start=(k == 0), stop=(k == 15))
                last.then_inc(tte, 1)


    @block.vector
    def _(v):
        for r in range(4):
            for j in range(16):
                jj = 16 * r + j
                v.wait_ge(sse, silb(r, j))
                if jj == 0:
                    v.wait_ge(tte, TT_P5A)
                if j == 0 and r > 0:
                    v.wait_ge(tte, dn_done(r - 1, 15))
                gub = (jj % 2) * 4
                for tt in range(NTT):
                    v.tensor_tensor(inter[:, j, tt * 512:(tt + 1) * 512],
                                    sd[:, tt * 512:(tt + 1) * 512], ps[gub + 2 + tt][:], ALU.mult)
                v.engine_nop().then_inc(vve, 1)
            for m in range(16):
                v.wait_ge(tte, dn_done(r, m))
                for tt in range(NTT):
                    v.tensor_add(mlpacc[:, m, tt * 512:(tt + 1) * 512],
                                 mlpacc[:, m, tt * 512:(tt + 1) * 512], ps[4 + (m % 2) * 2 + tt][:])
                v.engine_nop().then_inc(vve, 1)

    TT6, VVE6, SSE6 = TT_P5B, VVE_P5B, SSE_P5B

    @block.sync
    def _(sync):
        sync.wait_ge(s_sh, SH_ALL)
        for p in range(16):
            if p >= 2:
                sync.wait_ge(vve, VVE6 + (p - 2) + 1)
            sync.dma_start(out=x2[:, p % 2, :], in_=hfm_spill[p * 128:(p + 1) * 128, :]).then_inc(s_xr[p % 2], 16)

    @block.vector
    def _(v):
        for p in range(16):
            v.wait_ge(s_xr[p % 2], 8 * 32 + (p // 2 + 1) * 16)
            v.wait_ge(vve, inv_dn(3, p))
            v.tensor_add(mlpacc[:, p, :], mlpacc[:, p, :], x2[:, p % 2, :]).then_inc(vve, 1)
    VVE_RES2 = VVE6 + 16

    @block.scalar
    def _(s):
        for p in range(16):
            s.wait_ge(vve, VVE6 + p + 1)
            if p >= 2:
                s.wait_ge(tte, TT6 + (p - 2) + 1)
            s.activation(rsig[:, p % 2, :], mlpacc[:, p, :], AF.Square).then_inc(sse, 1)
    SSE_SQ2 = SSE6 + 16

    @block.tensor
    def _(t):
        t.wait_ge(vve, VVE_P5B)
        for p in range(16):
            t.wait_ge(sse, SSE6 + p + 1)
            last = None
            for tt in range(NTT):
                t.matmul(ps[4 + tt][0:1, :], s_ones[:], mlpacc[:, p, tt * 512:(tt + 1) * 512],
                         start=(p == 0), stop=(p == 15))
                last = t.matmul(ps[6 + tt][0:1, :], s_ones[:], rsig[:, p % 2, tt * 512:(tt + 1) * 512],
                                start=(p == 0), stop=(p == 15))
            last.then_inc(tte, 1)
    TT_ST2 = TT6 + 16

    @block.vector
    def _(v):
        v.wait_ge(tte, TT_ST2)
        v.tensor_copy(s_srow[:, 0:512], ps[4][0:1, :])
        v.tensor_copy(s_srow[:, 512:1024], ps[5][0:1, :])
        v.tensor_copy(s_srow[:, 1024:1536], ps[6][0:1, :])
        v.tensor_copy(s_srow[:, 1536:2048], ps[7][0:1, :])
        v.tensor_scalar_mul(s_mu[:], s_srow[:, 0:TOK], 1.0 / D)
        v.tensor_scalar_mul(s_inv[:], s_srow[:, TOK:2 * TOK], 1.0 / D)
        v.tensor_mul(s_srow[:, 0:TOK], s_mu[:], s_mu[:])
        v.tensor_sub(s_inv[:], s_inv[:], s_srow[:, 0:TOK])
        v.tensor_scalar_add(s_inv[:], s_inv[:], EPS).then_inc(vve, 1)
    VVE_VAR2 = VVE_RES2 + 1

    @block.scalar
    def _(s):
        s.wait_ge(vve, VVE_VAR2)
        s.activation(s_srow[:, 0:TOK], s_inv[:], AF.Sqrt).then_inc(sse, 1)
    SSE_SQRT2 = SSE_SQ2 + 1

    @block.vector
    def _(v):
        v.wait_ge(sse, SSE_SQRT2)
        v.reciprocal(s_inv[:], s_srow[:, 0:TOK]).then_inc(vve, 1)
    VVE_INV2 = VVE_VAR2 + 1

    @block.sync
    def _(sync):
        sync.wait_ge(vve, VVE_INV2)
        sync.dma_start(out=ln2_rows[0:1, :], in_=s_mu[:]).then_inc(s_sq, 16)
        sync.dma_start(out=ln2_rows[1:2, :], in_=s_inv[:]).then_inc(s_sq, 16)
        sync.wait_ge(s_sq, SQ_CCQ + 32)
        sync.dma_start(out=bcst[:, 0, :], in_=ln2_rows[0:1, :].broadcast_to([128, TOK])).then_inc(s_sq, 16)
        sync.dma_start(out=bcst[:, 1, :], in_=ln2_rows[1:2, :].broadcast_to([128, TOK])).then_inc(s_sq, 16)
    SQ_BC2 = SQ_CCQ + 4 * 16

    @block.vector
    def _(v):
        v.wait_ge(s_sq, SQ_BC2)
        for p in range(16):
            v.tensor_sub(mlpacc[:, p, :], mlpacc[:, p, :], bcst[:, 0, :])
            v.tensor_mul(mlpacc[:, p, :], mlpacc[:, p, :], bcst[:, 1, :]).then_inc(vve, 1)
    VVE_OUT = VVE_INV2 + 16

    @block.sync
    def _(sync):
        for p in range(16):
            sync.wait_ge(vve, VVE_INV2 + p + 1)
            sync.dma_start(out=out_o[p * 128:(p + 1) * 128, :], in_=mlpacc[:, p, :]).then_inc(s_out, 16)
        sync.wait_ge(s_out, 16 * 16)

    return nc


def _packT(w, s=None):
    co, ci = w.shape
    nj, nk = co // 128, ci // 128
    a = w.reshape(nj, 128, nk, 128)
    a = a.transpose(0, 3, 2, 1)
    return np.ascontiguousarray(a.reshape(nj, 128, nk * 128))


def _pack_nat(w):
    ci, co = w.shape
    nk, nj = ci // 128, co // 128
    a = w.reshape(nk, 128, nj, 128)
    a = a.transpose(2, 1, 0, 3)
    return np.ascontiguousarray(a.reshape(nj, 128, nk * 128))


_CACHE = {}


def kernel(L0_attn_x, L0_attn_W_act, L0_attn_W_val, L0_attn_W_rec, L0_ffn_W_q,
           L0_ffn_keys, L0_ffn_down_embed, L0_ffn_up_embed, L0_ffn_gate_w,
           L0_ffn_up_w, L0_ffn_down_w):
    x = np.asarray(L0_attn_x, dtype=np.float32)
    W_act = np.asarray(L0_attn_W_act, np.float32)
    W_val = np.asarray(L0_attn_W_val, np.float32)
    W_rec = np.asarray(L0_attn_W_rec, np.float32)
    W_q = np.asarray(L0_ffn_W_q, np.float32)
    keys = np.asarray(L0_ffn_keys, np.float32)
    de = np.asarray(L0_ffn_down_embed, np.float32)
    ue = np.asarray(L0_ffn_up_embed, np.float32)
    gw = np.asarray(L0_ffn_gate_w, np.float32)
    uw = np.asarray(L0_ffn_up_w, np.float32)
    dw = np.asarray(L0_ffn_down_w, np.float32)

    perm = np.concatenate([np.arange(0, D, 2), np.arange(1, D, 2)])
    shared = dict(
        wact_re=_packT(W_act[0::2], None),
        wact_im=_packT(W_act[1::2], None),
        wrec=_packT(W_rec),
        wval=_packT(np.ascontiguousarray(W_val[:, perm])),
        wq=_pack_nat(W_q),
        keys_in=np.ascontiguousarray(keys),
        de_pk=_packT(de),
        ue_pk=_pack_nat(ue),
        gate_pk=_pack_nat(gw),
        up_pk=_pack_nat(uw),
        down_pk=_pack_nat(dw),
        identb=np.eye(128, dtype=ml_dtypes.bfloat16),
        onesb=np.ones((128, 1), dtype=np.float32),
        iota64=np.broadcast_to(np.arange(64, dtype=np.float32), (128, 64)).copy(),
    )

    in_maps = []
    for c in range(NCORE):
        b, q = c // 4, c % 4
        s0 = TOK * q
        xs = x[b, s0:s0 + TOK, :]
        halo = x[b, s0 - 1, :] if q > 0 else np.zeros(D, np.float32)
        xf = np.empty((NCH, 128, TOK + 1), np.float32)
        xt = np.ascontiguousarray(xs.T)
        for k in range(NCH):
            xf[k, :, 0] = halo[k * 128:(k + 1) * 128]
            xf[k, :, 1:] = xt[k * 128:(k + 1) * 128, :]
        mm = np.full((128, 8), -1e30, np.float32)
        ma = np.zeros((128, 8), np.float32)
        for r in range(4 * b, c):
            mm[:, r] = 0.0
            ma[:, r] = 1.0
        me3 = np.zeros((64, 32), np.float32)
        me3[:, 0 * 16 + (c // 2) * 2 + (c % 2)] = 1.0
        me3[:, 1 * 16 + (4 + c // 2) * 2 + (c % 2)] = 1.0
        im = dict(shared)
        im.update(xfm=xf, mask_max=mm, mask_add=ma, mask_e3=me3)
        in_maps.append(im)

    nc = build_nc()
    lower_extended_insts(nc)
    res = run_bass_kernel_spmd(nc, in_maps, core_ids=list(range(NCORE)))

    out = np.empty((2, 4096, D), np.float32)
    for c in range(NCORE):
        b, q = c // 4, c % 4
        out[b, TOK * q:TOK * (q + 1), :] = np.asarray(res.results[c]["o"]).T
    return out


# version 04
# speedup vs baseline: 1.0498x; 1.0498x over previous
import math
import numpy as np
import ml_dtypes
import concourse.bass as bass
import concourse.mybir as mybir
from concourse.bass_utils import run_bass_kernel_spmd
from concourse.library_overlay import lower_extended_insts
from concourse import library_config
from contextlib import ExitStack

F32 = mybir.dt.float32
BF16 = mybir.dt.bfloat16
I16 = mybir.dt.int16
I32 = mybir.dt.int32
AF = mybir.ActivationFunctionType
ALU = mybir.AluOpType
AX = mybir.AxisListType

NCORE = 8
TOK = 1024
D = 2048
NCH = 16
NTT = 2
TWO_PI = 2.0 * math.pi
EPS = 1e-5

DEBUG_TAPS = []


def build_nc(dev=False):
    devk = dict(kind="ExternalOutput") if dev else {}
    nc = bass.Bass("TRN2", target_bir_lowering=False, num_devices=NCORE,
                   detect_race_conditions=False)

    xfm = nc.dram_tensor("xfm", [NCH, 128, TOK + 1], F32, kind="ExternalInput").ap()
    wact_re = nc.dram_tensor("wact_re", [8, 128, 32 * 128], F32, kind="ExternalInput").ap()
    wact_im = nc.dram_tensor("wact_im", [8, 128, 32 * 128], F32, kind="ExternalInput").ap()
    wrec = nc.dram_tensor("wrec", [16, 128, 32 * 128], F32, kind="ExternalInput").ap()
    wval = nc.dram_tensor("wval", [16, 128, 16 * 128], F32, kind="ExternalInput").ap()
    wq = nc.dram_tensor("wq", [1, 128, 16 * 128], F32, kind="ExternalInput").ap()
    keys_in = nc.dram_tensor("keys_in", [2, 64, 64], F32, kind="ExternalInput").ap()
    de_pk = nc.dram_tensor("de_pk", [32, 128, 16 * 128], BF16, kind="ExternalInput").ap()
    ue_pk = nc.dram_tensor("ue_pk", [16, 128, 32 * 128], BF16, kind="ExternalInput").ap()
    gate_pk = nc.dram_tensor("gate_pk", [64, 128, 16 * 128], BF16, kind="ExternalInput").ap()
    up_pk = nc.dram_tensor("up_pk", [64, 128, 16 * 128], BF16, kind="ExternalInput").ap()
    down_pk = nc.dram_tensor("down_pk", [16, 128, 64 * 128], BF16, kind="ExternalInput").ap()
    identb = nc.dram_tensor("identb", [128, 128], BF16, kind="ExternalInput").ap()
    onesb = nc.dram_tensor("onesb", [128, 1], F32, kind="ExternalInput").ap()
    iota64 = nc.dram_tensor("iota64", [128, 64], F32, kind="ExternalInput").ap()
    mask_max = nc.dram_tensor("mask_max", [128, 8], F32, kind="ExternalInput").ap()
    mask_add = nc.dram_tensor("mask_add", [128, 8], F32, kind="ExternalInput").ap()
    mask_e3 = nc.dram_tensor("mask_e3", [64, 32], F32, kind="ExternalInput").ap()

    out_o = nc.dram_tensor("o", [D, TOK], F32, kind="ExternalOutput").ap()

    r4_spill = nc.dram_tensor("r4_spill", [NCH * 128, TOK], F32, **devk).ap()
    phi_spill = nc.dram_tensor("phi_spill", [8 * 128, TOK], F32, **devk).ap()
    hfm_spill = nc.dram_tensor("hfm_spill", [NCH * 128, TOK], F32, **devk).ap()
    kv_dump = nc.dram_tensor("kv_dump", [16 * 128, TOK], BF16, **devk).ap() if dev else None
    q_dump = nc.dram_tensor("q_dump", [64, 2 * TOK], BF16, **devk).ap() if dev else None
    qb_dump = nc.dram_tensor("qb_dump", [64, 2 * TOK], BF16, **devk).ap() if dev else None
    sm_dump = nc.dram_tensor("sm_dump", [128, 64], BF16, **devk).ap() if dev else None
    iq_dump = nc.dram_tensor("iq_dump", [128, 256], BF16, **devk).ap() if dev else None
    exp_dump = nc.dram_tensor("exp_dump", [NCH * 128, TOK], F32, **devk).ap() if dev else None
    tk_dump = nc.dram_tensor("tk_dump", [128, 320], F32, **devk).ap() if dev else None
    macc_dump = nc.dram_tensor("macc_dump", [NCH * 128, TOK], F32, **devk).ap() if dev else None
    ln1_rows = nc.dram_tensor("ln1_rows", [2, TOK], F32).ap()
    ln2_rows = nc.dram_tensor("ln2_rows", [2, TOK], F32).ap()
    cc1_in = nc.dram_tensor("cc1_in", [128, 16], F32).ap()
    cc1_dump = nc.dram_tensor("cc1_dump", [128, 16], F32, **devk).ap() if dev else None
    cc1_out = nc.dram_tensor("cc1_out", [NCORE * 128, 16], F32, addr_space="Shared").ap()
    cc2_in = nc.dram_tensor("cc2_in", [128, 8], F32).ap()
    cc2_out = nc.dram_tensor("cc2_out", [NCORE * 128, 8], F32, addr_space="Shared").ap()
    cc3_in = nc.dram_tensor("cc3_in", [64, 2 * TOK], BF16).ap()
    cc3_out = nc.dram_tensor("cc3_out", [NCORE * 64, 2 * TOK], BF16, addr_space="Shared").ap()

    ctx = ExitStack()
    E = ctx.enter_context

    s_iota = E(nc.sbuf_tensor("s_iota", [128, 64], F32))
    s_mmax = E(nc.sbuf_tensor("s_mmax", [128, 8], F32))
    s_madd = E(nc.sbuf_tensor("s_madd", [128, 8], F32))
    s_me3 = E(nc.sbuf_tensor("s_me3", [64, 32], F32))
    s_idb = E(nc.sbuf_tensor("s_idb", [128, 128], BF16))
    s_ones = E(nc.sbuf_tensor("s_ones", [128, 1], F32))
    s_keys = E(nc.sbuf_tensor("s_keys", [64, 2, 64], BF16))
    wsl = [E(nc.sbuf_tensor(f"wsl{i}", [128, 32 * 128], BF16)) for i in range(2)]
    s_agg = E(nc.sbuf_tensor("s_agg", [128, 16], F32))
    s_ag1 = E(nc.sbuf_tensor("s_ag1", [128, 8, 16], F32))
    s_agg2 = E(nc.sbuf_tensor("s_agg2", [128, 8], F32))
    s_ag2 = E(nc.sbuf_tensor("s_ag2", [128, 8, 8], F32))
    s_prev1 = E(nc.sbuf_tensor("s_prev1", [128, 16], F32))
    s_prev2 = E(nc.sbuf_tensor("s_prev2", [128, 8], F32))
    s_tmp38 = E(nc.sbuf_tensor("s_tmp38", [128, 16, 8], F32))
    s_srow = E(nc.sbuf_tensor("s_srow", [1, 2 * TOK], F32))
    s_mu = E(nc.sbuf_tensor("s_mu", [1, TOK], F32))
    s_inv = E(nc.sbuf_tensor("s_inv", [1, TOK], F32))
    s_sm = E(nc.sbuf_tensor("s_sm", [128, 8, 8], BF16))
    s_iq4 = E(nc.sbuf_tensor("s_iq4", [128, 8, 4, 8], I16))
    s_neg1 = E(nc.sbuf_tensor("s_neg1", [128, 8], F32))
    s_zero = E(nc.sbuf_tensor("s_zero", [128, 64], F32))
    s_topk = E(nc.sbuf_tensor("s_topk", [128, 64], F32))
    s_topk2 = E(nc.sbuf_tensor("s_topk2", [128, 64], F32))
    s_t8 = E(nc.sbuf_tensor("s_t8", [128, 4, 8], F32))
    s_i8 = E(nc.sbuf_tensor("s_i8", [128, 4, 8], F32))
    s_eq = E(nc.sbuf_tensor("s_eq", [128, 8, 64], F32))
    s_a64 = E(nc.sbuf_tensor("s_a64", [128, 2, 64], F32))
    s_sm1 = E(nc.sbuf_tensor("s_sm1", [128, 8], F32))
    s_ssum = E(nc.sbuf_tensor("s_ssum", [128, 1], F32))
    cm1f = E(nc.sbuf_tensor("cm1f", [128, 1], F32))
    nc.const_aps.register(F32, -1.0, cm1f[:])
    chpi = E(nc.sbuf_tensor("chpi", [128, 1], F32))
    nc.const_aps.register(F32, math.pi / 2, chpi[:])
    chf = E(nc.sbuf_tensor("chf", [128, 1], F32))
    nc.const_aps.register(F32, 0.5, chf[:])

    ps = [E(nc.psum_tensor(f"ps{i}", [128, 512], F32)) for i in range(8)]

    s_cn = E(nc.semaphore("s_cn"))
    s_sq = E(nc.semaphore("s_sq"))
    s_x = E(nc.semaphore("s_x"))
    s_w = [E(nc.semaphore(f"s_w{i}")) for i in range(2)]
    s_pp = [E(nc.semaphore(f"s_pp{i}")) for i in range(2)]
    s_pr = [E(nc.semaphore(f"s_pr{i}")) for i in range(2)]
    s_sp = [E(nc.semaphore(f"s_sp{i}")) for i in range(2)]
    s_xr = [E(nc.semaphore(f"s_xr{i}")) for i in range(2)]
    s_sh = E(nc.semaphore("s_sh"))
    s_out = E(nc.semaphore("s_out"))
    s_vb = E(nc.semaphore("s_vb"))
    _vbn = [0]

    def VB(ins, v):
        ins.then_inc(s_vb, 1)
        _vbn[0] += 1
        v.wait_ge(s_vb, _vbn[0])
        return ins

    lcc = E(nc.semaphore("lcc"))
    tte = E(nc.semaphore("tte"))
    vve = E(nc.semaphore("vve"))
    sse = E(nc.semaphore("sse"))
    gge = E(nc.semaphore("gge"))

    block = E(nc.Block())

    g_kvbf = nc.sbuf_tensor("kvbf", [128, NCH, TOK], BF16, side="right"); kvbf = g_kvbf.__enter__()
    g_rsig = nc.sbuf_tensor("rsig", [128, 2, TOK], F32, side="right"); rsig = g_rsig.__enter__()
    g_xbf = nc.sbuf_tensor("xbf", [128, NCH, TOK + 1], BF16); xbf = g_xbf.__enter__()
    g_mag2 = nc.sbuf_tensor("mag2", [128, 8, TOK], F32); mag2 = g_mag2.__enter__()
    g_trans = nc.sbuf_tensor("trans", [128, 8, TOK], F32); trans = g_trans.__enter__()
    g_rtmp = nc.sbuf_tensor("rtmp", [128, TOK], F32); rtmp = g_rtmp.__enter__()
    g_rst = nc.sbuf_tensor("rst", [128, 2, TOK], F32); rst = g_rst.__enter__()
    g_phb = nc.sbuf_tensor("phb", [128, 2, TOK], F32); phb = g_phb.__enter__()

    @block.sync
    def _(sync):
        sync.dma_start(out=s_iota[:], in_=iota64).then_inc(s_cn, 16)
        sync.dma_start(out=s_mmax[:], in_=mask_max).then_inc(s_cn, 16)
        sync.dma_start(out=s_madd[:], in_=mask_add).then_inc(s_cn, 16)
        sync.dma_start(out=s_me3[:], in_=mask_e3).then_inc(s_cn, 16)
        sync.dma_start(out=s_idb[:], in_=identb).then_inc(s_cn, 16)
        sync.dma_start(out=s_ones[:], in_=onesb).then_inc(s_cn, 16)
    CN_ALL = 6 * 16

    @block.gpsimd
    def _(g):
        g.load_library(library_config.local_scatter)
        for k in range(NCH):
            g.dma_start(out=xbf[:, k, :], in_=xfm[k]).then_inc(s_x, 16)
        g.dma_start(out=s_keys[0:64, 0, :], in_=keys_in[0]).then_inc(s_x, 16)
        g.dma_start(out=s_keys[0:64, 1, :], in_=keys_in[1]).then_inc(s_x, 16)
    X_ALL = 18 * 16

    @block.vector
    def _(v):
        v.memset(cm1f[:], -1.0)
        v.memset(chpi[:], math.pi / 2)
        v.memset(chf[:], 0.5)
        v.memset(s_neg1[:], -1.0)
        v.memset(s_zero[:], 0.0).then_inc(vve, 1)
    VVE0 = 1

    TT1, SSE1, VVE1 = 0, 0, VVE0

    def re_t(j): return trans[:, (j % 2) * 4 + 0, :]
    def im_t(j): return trans[:, (j % 2) * 4 + 1, :]
    def sq1_t(j): return trans[:, (j % 2) * 4 + 2, :]
    def sq2_t(j): return trans[:, (j % 2) * 4 + 3, :]

    @block.gpsimd
    def _(g):
        for p in range(16):
            if p >= 2:
                g.wait_ge(tte, TT1 + p - 1)
            srcp = (wact_re if p % 2 == 0 else wact_im)[p // 2]
            g.dma_start(out=wsl[p % 2][:], in_=srcp).then_inc(s_w[p % 2], 16)

    @block.tensor
    def _(t):
        t.wait_ge(s_x, X_ALL)
        for p in range(16):
            t.wait_ge(s_w[p % 2], (p // 2 + 1) * 16)
            if p >= 2:
                jp, pl = (p - 2) // 2, (p - 2) % 2
                t.wait_ge(sse, SSE1 + 4 * jp + 1 + pl)
            last = None
            for k in range(32):
                ch = k if k < 16 else k - 16
                off = 0 if k < 16 else 1
                for tt in range(NTT):
                    last = t.matmul(ps[(p % 2) * 2 + tt][:],
                                    wsl[p % 2][:, k * 128:(k + 1) * 128],
                                    xbf[:, ch, off + tt * 512: off + tt * 512 + 512],
                                    start=(k == 0), stop=(k == 31))
            last.then_inc(tte, 1)
    TT_P1 = TT1 + 16

    @block.scalar
    def _(s):
        for j in range(8):
            if j >= 2:
                s.wait_ge(vve, VVE1 + 3 * (j - 2) + 3)
            s.wait_ge(tte, TT1 + 2 * j + 1)
            s.activation(re_t(j)[:, 0:512], ps[(2 * j % 2) * 2 + 0][:], AF.Sigmoid)
            s.activation(re_t(j)[:, 512:1024], ps[1][:], AF.Sigmoid).then_inc(sse, 1)
            s.wait_ge(tte, TT1 + 2 * j + 2)
            s.activation(im_t(j)[:, 0:512], ps[2][:], AF.Sigmoid)
            s.activation(im_t(j)[:, 512:1024], ps[3][:], AF.Sigmoid).then_inc(sse, 1)
            s.wait_ge(vve, VVE1 + 3 * j + 1)
            s.activation(sq1_t(j)[:], re_t(j)[:], AF.Square)
            s.activation(sq2_t(j)[:], im_t(j)[:], AF.Square).then_inc(sse, 1)
            s.wait_ge(vve, VVE1 + 3 * j + 2)
            if j >= 2:
                s.wait_ge(s_pp[j % 2], (j // 2) * 16)
            s.activation(phb[:, j % 2, :], sq2_t(j)[:], AF.Arctan).then_inc(sse, 1)
    SSE_P1 = SSE1 + 32

    @block.vector
    def _(v):
        for j in range(8):
            v.wait_ge(sse, SSE1 + 4 * j + 2)
            v.tensor_scalar_max(re_t(j)[:], re_t(j)[:], 0.5)
            v.tensor_scalar_max(im_t(j)[:], im_t(j)[:], 0.5).then_inc(vve, 1)
            v.wait_ge(sse, SSE1 + 4 * j + 3)
            v.tensor_add(mag2[:, j, :], sq1_t(j)[:], sq2_t(j)[:])
            v.reciprocal(sq1_t(j)[:], re_t(j)[:])
            v.tensor_mul(sq1_t(j)[:], im_t(j)[:], sq1_t(j)[:])
            v.reciprocal(sq2_t(j)[:], im_t(j)[:])
            v.tensor_mul(sq2_t(j)[:], re_t(j)[:], sq2_t(j)[:])
            v.tensor_tensor(sq2_t(j)[:], sq1_t(j)[:], sq2_t(j)[:], ALU.min)
            v.tensor_reduce(s_agg[:, j:j + 1], mag2[:, j, :], AX.X, ALU.max).then_inc(vve, 1)
            v.wait_ge(sse, SSE1 + 4 * j + 4)
            v.tensor_scalar(sq2_t(j)[:], sq1_t(j)[:], 1.0, None, ALU.is_le)
            v.tensor_scalar(sq2_t(j)[:], sq2_t(j)[:], 2.0, -1.0, ALU.mult, ALU.add)
            v.tensor_mul(phb[:, j % 2, :], phb[:, j % 2, :], sq2_t(j)[:])
            v.tensor_scalar(sq1_t(j)[:], sq2_t(j)[:], -math.pi / 4, math.pi / 4, ALU.mult, ALU.add)
            v.tensor_add(phb[:, j % 2, :], phb[:, j % 2, :], sq1_t(j)[:])
            v.tensor_reduce(s_agg[:, 8 + j:8 + j + 1], phb[:, j % 2, :], AX.X, ALU.add).then_inc(vve, 1)
    VVE_P1 = VVE1 + 24

    @block.sync
    def _(sync):
        for j in range(8):
            sync.wait_ge(vve, VVE1 + 3 * j + 3)
            sync.dma_start(out=phi_spill[j * 128:(j + 1) * 128, :], in_=phb[:, j % 2, :]).then_inc(s_pp[j % 2], 16)

    @block.sync
    def _(sync):
        sync.wait_ge(vve, VVE_P1)
        sync.dma_start(out=cc1_in, in_=s_agg[:]).then_inc(s_sq, 16)
    SQ_CC1IN = 16

    @block.gpsimd
    def _(g):
        g.wait_ge(s_sq, SQ_CC1IN)
        g.collective_compute("AllGather", ALU.bypass, replica_groups=[list(range(NCORE))],
                             ins=[cc1_in.opt()], outs=[cc1_out.opt()]).then_inc(lcc, 1)
    LCC1 = 1

    @block.sync
    def _(sync):
        sync.wait_ge(lcc, LCC1)
        for r in range(8):
            sync.dma_start(out=s_ag1[:, r, :], in_=cc1_out[r * 128:(r + 1) * 128, :]).then_inc(s_sq, 16)
    SQ_AG1 = SQ_CC1IN + 8 * 16

    TT2, SSE2, VVE2 = TT_P1, SSE_P1, VVE_P1

    @block.gpsimd
    def _(g):
        for p in range(16):
            g.wait_ge(tte, TT2 - 2 + p + 1)
            g.dma_start(out=wsl[p % 2][:], in_=wrec[p]).then_inc(s_w[p % 2], 16)

    @block.tensor
    def _(t):
        for p in range(16):
            t.wait_ge(s_w[p % 2], (8 + p // 2 + 1) * 16)
            if p >= 2:
                t.wait_ge(sse, SSE2 + (p - 2) + 1)
            last = None
            for k in range(32):
                ch = k if k < 16 else k - 16
                off = 0 if k < 16 else 1
                for tt in range(NTT):
                    last = t.matmul(ps[(p % 2) * 2 + tt][:],
                                    wsl[p % 2][:, k * 128:(k + 1) * 128],
                                    xbf[:, ch, off + tt * 512: off + tt * 512 + 512],
                                    start=(k == 0), stop=(k == 31))
            last.then_inc(tte, 1)
    TT_P2 = TT2 + 16

    @block.scalar
    def _(s):
        for p in range(16):
            s.wait_ge(tte, TT2 + p + 1)
            if p >= 2:
                s.wait_ge(vve, VVE2 + (p - 2) + 1)
            s.activation(rsig[:, p % 2, 0:512], ps[(p % 2) * 2 + 0][:], AF.Sigmoid)
            s.activation(rsig[:, p % 2, 512:1024], ps[(p % 2) * 2 + 1][:], AF.Sigmoid).then_inc(sse, 1)
    SSE_P2R = SSE2 + 16

    @block.vector
    def _(v):
        for p in range(16):
            v.wait_ge(sse, SSE2 + p + 1)
            if p >= 2:
                v.wait_ge(s_sp[p % 2], (p // 2) * 16)
            v.tensor_scalar_max(rsig[:, p % 2, :], rsig[:, p % 2, :], 0.5)
            v.tensor_mul(rtmp[:], rsig[:, p % 2, :], rsig[:, p % 2, :])
            v.tensor_mul(rst[:, p % 2, :], rtmp[:], rtmp[:]).then_inc(vve, 1)
    VVE_P2R = VVE2 + 16

    @block.sync
    def _(sync):
        for p in range(16):
            sync.wait_ge(vve, VVE2 + p + 1)
            sync.dma_start(out=r4_spill[p * 128:(p + 1) * 128, :], in_=rst[:, p % 2, :]).then_inc(s_sp[p % 2], 16)

    VVE_RSB = VVE_P2R
    SSE_RSB = SSE_P2R

    @block.vector
    def _(v):
        v.wait_ge(s_sq, SQ_AG1)
        v.wait_ge(s_cn, CN_ALL)
        a3 = s_ag1[:].transpose([0, 2, 1])
        m3m = s_mmax[:].unsqueeze(1).broadcast_to([128, 16, 8])
        m3a = s_madd[:].unsqueeze(1).broadcast_to([128, 16, 8])
        VB(v.tensor_tensor(s_tmp38[:], a3, m3m, ALU.add), v)
        VB(v.tensor_reduce(s_prev1[:, 0:8].unsqueeze(2), s_tmp38[:, 0:8, :], AX.X, ALU.max), v)
        VB(v.tensor_tensor(s_tmp38[:], a3, m3a, ALU.mult), v)
        VB(v.tensor_reduce(s_prev1[:, 8:16].unsqueeze(2), s_tmp38[:, 8:16, :], AX.X, ALU.add), v)
        v.engine_nop().then_inc(vve, 1)
        for j in range(8):
            if j >= 2:
                v.wait_ge(sse, SSE_RSB + (j - 2) + 1)
            v.tensor_tensor_scan(trans[:, (j % 2) * 3 + 0, :], mag2[:, j, :], mag2[:, j, :],
                                 s_prev1[:, j:j + 1], ALU.max, ALU.max).then_inc(vve, 1)
            v.wait_ge(sse, SSE_RSB + j + 1)
            v.tensor_sub(mag2[:, j, :], trans[:, (j % 2) * 3 + 1, :], trans[:, (j % 2) * 3 + 2, :])
            v.tensor_reduce(s_agg2[:, j:j + 1], mag2[:, j, :], AX.X, ALU.add).then_inc(vve, 1)
    VVE_RS = VVE_RSB + 1 + 16

    @block.scalar
    def _(s):
        for j in range(8):
            s.wait_ge(vve, VVE_RSB + 1 + 2 * j + 1)
            s.activation(trans[:, (j % 2) * 3 + 1, :], mag2[:, j, :], AF.Ln)
            s.activation(trans[:, (j % 2) * 3 + 2, :], trans[:, (j % 2) * 3 + 0, :], AF.Ln).then_inc(sse, 1)
    SSE_RS = SSE_RSB + 8

    @block.sync
    def _(sync):
        sync.wait_ge(vve, VVE_RS)
        sync.dma_start(out=cc2_in, in_=s_agg2[:]).then_inc(s_sq, 16)
    SQ_CC2IN = SQ_AG1 + 16

    @block.gpsimd
    def _(g):
        g.wait_ge(s_sq, SQ_CC2IN)
        g.collective_compute("AllGather", ALU.bypass, replica_groups=[list(range(NCORE))],
                             ins=[cc2_in.opt()], outs=[cc2_out.opt()]).then_inc(lcc, 1)
    LCC2 = 2

    @block.sync
    def _(sync):
        sync.wait_ge(lcc, LCC2)
        for r in range(8):
            sync.dma_start(out=s_ag2[:, r, :], in_=cc2_out[r * 128:(r + 1) * 128, :]).then_inc(s_sq, 16)
        for j in range(8):
            if j >= 2:
                sync.wait_ge(vve, VVE_RS + 1 + 3 * (j - 2) + 3)
            sync.wait_ge(s_pp[j % 2], (j // 2 + 1) * 16)
            sync.dma_start(out=phb[:, j % 2, :], in_=phi_spill[j * 128:(j + 1) * 128, :]).then_inc(s_pr[j % 2], 16)
    SQ_AG2 = SQ_CC2IN + 8 * 16

    VVE_FB = VVE_RS
    SSE_FB = SSE_RS

    @block.vector
    def _(v):
        v.wait_ge(s_sq, SQ_AG2)
        a2 = s_ag2[:].transpose([0, 2, 1])
        m3a = s_madd[:].unsqueeze(1).broadcast_to([128, 8, 8])
        VB(v.tensor_tensor(s_tmp38[:, 0:8, :], a2, m3a, ALU.mult), v)
        VB(v.tensor_reduce(s_prev2[:].unsqueeze(2), s_tmp38[:, 0:8, :], AX.X, ALU.add), v)
        v.engine_nop().then_inc(vve, 1)
        for j in range(8):
            if j >= 1:
                v.wait_ge(sse, SSE_FB + 2 * (j - 1) + 1)
            v.wait_ge(s_pr[j % 2], (j // 2 + 1) * 16)
            v.tensor_tensor_scan(trans[:, 0, :], mag2[:, j, :], s_zero[:, 0:1].broadcast_to([128, TOK]),
                                 s_prev2[:, j:j + 1], ALU.add, ALU.add)
            v.tensor_tensor_scan(trans[:, 1, :], phb[:, j % 2, :], s_zero[:, 0:1].broadcast_to([128, TOK]),
                                 s_prev1[:, 8 + j:8 + j + 1], ALU.add, ALU.add).then_inc(vve, 1)
            v.tensor_scalar_mul(trans[:, 2, :], trans[:, 1, :], 1.0 / TWO_PI)
            v.tensor_copy(trans[:, 3, :].bitcast(I32), trans[:, 2, :])
            v.tensor_copy(trans[:, 2, :], trans[:, 3, :].bitcast(I32))
            v.scalar_tensor_tensor(trans[:, 3, :], trans[:, 2, :], -TWO_PI, trans[:, 1, :], ALU.mult, ALU.add)
            v.tensor_scalar(trans[:, 2, :], trans[:, 3, :], math.pi, None, ALU.is_gt)
            v.scalar_tensor_tensor(trans[:, 3, :], trans[:, 2, :], -TWO_PI, trans[:, 3, :], ALU.mult, ALU.add)
            v.tensor_scalar(trans[:, 2, :], trans[:, 3, :], -math.pi, None, ALU.is_lt)
            v.scalar_tensor_tensor(trans[:, 3, :], trans[:, 2, :], TWO_PI, trans[:, 3, :], ALU.mult, ALU.add)
            v.tensor_scalar_max(trans[:, 3, :], trans[:, 3, :], -math.pi)
            v.tensor_scalar_min(trans[:, 3, :], trans[:, 3, :], math.pi)
            v.tensor_scalar(trans[:, 4, :], trans[:, 1, :], 1.0 / TWO_PI, 0.25, ALU.mult, ALU.add)
            v.tensor_copy(trans[:, 5, :].bitcast(I32), trans[:, 4, :])
            v.tensor_copy(trans[:, 4, :], trans[:, 5, :].bitcast(I32))
            v.scalar_tensor_tensor(trans[:, 5, :], trans[:, 4, :], -TWO_PI, trans[:, 1, :], ALU.mult, ALU.add)
            v.tensor_scalar(trans[:, 4, :], trans[:, 5, :], 0.5 * math.pi, None, ALU.is_gt)
            v.scalar_tensor_tensor(trans[:, 5, :], trans[:, 4, :], -TWO_PI, trans[:, 5, :], ALU.mult, ALU.add)
            v.tensor_scalar(trans[:, 4, :], trans[:, 5, :], -1.5 * math.pi, None, ALU.is_lt)
            v.scalar_tensor_tensor(trans[:, 5, :], trans[:, 4, :], TWO_PI, trans[:, 5, :], ALU.mult, ALU.add)
            v.tensor_scalar_max(trans[:, 5, :], trans[:, 5, :], -1.5 * math.pi)
            v.tensor_scalar_min(trans[:, 5, :], trans[:, 5, :], 0.5 * math.pi).then_inc(vve, 1)
            v.wait_ge(sse, SSE_FB + 2 * j + 2)
            v.tensor_mul(kvbf[:, j, :], trans[:, 6, :], trans[:, 2, :])
            v.tensor_mul(kvbf[:, 8 + j, :], trans[:, 6, :], trans[:, 7, :]).then_inc(vve, 1)
    VVE_KV = VVE_FB + 1 + 24

    @block.scalar
    def _(s):
        for j in range(8):
            s.wait_ge(vve, VVE_FB + 1 + 3 * j + 1)
            s.activation(trans[:, 6, :], trans[:, 0, :], AF.Exp, scale=0.5).then_inc(sse, 1)
            s.wait_ge(vve, VVE_FB + 1 + 3 * j + 2)
            if j >= 1:
                s.wait_ge(vve, VVE_FB + 1 + 3 * (j - 1) + 3)
            s.activation(trans[:, 7, :], trans[:, 3, :], AF.Sin)
            s.activation(trans[:, 2, :], trans[:, 5, :], AF.Sin, bias=math.pi / 2).then_inc(sse, 1)
    SSE_KV = SSE_FB + 16

    if dev:
        s_dev = nc.semaphore("s_dev").__enter__()

        @block.sync
        def _(sync):
            sync.dma_start(out=cc1_dump, in_=s_agg[:]).then_inc(s_dev, 16)
            for j in range(8):
                sync.wait_ge(vve, VVE_FB + 1 + 3 * j + 3)
                sync.dma_start(out=kv_dump[j * 128:(j + 1) * 128, :], in_=kvbf[:, j, :]).then_inc(s_dev, 16)
                sync.dma_start(out=kv_dump[(8 + j) * 128:(9 + j) * 128, :], in_=kvbf[:, 8 + j, :]).then_inc(s_dev, 16)
            sync.wait_ge(s_dev, 272)

    for _g in (g_phb, g_rst, g_rtmp, g_trans, g_mag2, g_xbf):
        _g.__exit__(None, None, None)
    bcst = nc.sbuf_tensor("bcast", [128, 2, TOK], F32).__enter__()
    x2 = nc.sbuf_tensor("x2", [128, 2, TOK], F32).__enter__()
    r42 = nc.sbuf_tensor("r42", [128, 2, TOK], F32).__enter__()
    g_hbf = nc.sbuf_tensor("hbf", [128, NCH, TOK], BF16); hbf = g_hbf.__enter__()
    g_hpre = nc.sbuf_tensor("hpre", [128, NCH, TOK], F32); hpre = g_hpre.__enter__()

    TT3, SSE3, VVE3 = TT_P2, SSE_KV, VVE_KV

    def att_done(p):
        return TT3 + p + 1 if p < 2 else TT3 + 2 * p - 1

    def stats1_done(sp):
        return TT3 + 2 * sp + 4 if sp <= 13 else TT3 + 31 + (sp - 14)

    @block.gpsimd
    def _(g):
        for p in range(16):
            if p >= 2:
                g.wait_ge(tte, att_done(p - 2))
            g.dma_start(out=wsl[p % 2][:, 0:16 * 128], in_=wval[p]).then_inc(s_w[p % 2], 16)

    @block.sync
    def _(sync):
        for p in range(16):
            if p >= 2:
                sync.wait_ge(vve, VVE3 + (p - 2) + 1)
            sync.dma_start(out=x2[:, p % 2, :], in_=xfm[p][:, 1:TOK + 1]).then_inc(s_xr[p % 2], 16)
            sync.wait_ge(s_sp[p % 2], (p // 2 + 1) * 16)
            sync.dma_start(out=r42[:, p % 2, :], in_=r4_spill[p * 128:(p + 1) * 128, :]).then_inc(s_xr[p % 2], 16)

    @block.tensor
    def _(t):
        t.wait_ge(vve, VVE_KV)
        t.wait_ge(sse, SSE_P2R)
        t.wait_ge(s_cn, CN_ALL)

        def emit_stats1(sp):
            t.wait_ge(sse, SSE3 + sp + 1)
            lastm = None
            for tt in range(NTT):
                t.matmul(ps[4 + tt][0:1, :], s_ones[:], hpre[:, sp, tt * 512:(tt + 1) * 512],
                         start=(sp == 0), stop=(sp == 15))
                lastm = t.matmul(ps[6 + tt][0:1, :], s_ones[:], rsig[:, sp % 2, tt * 512:(tt + 1) * 512],
                                 start=(sp == 0), stop=(sp == 15))
            lastm.then_inc(tte, 1)

        for p in range(16):
            t.wait_ge(s_w[p % 2], (16 + p // 2 + 1) * 16)
            if p >= 2:
                t.wait_ge(vve, VVE3 + (p - 2) + 1)
            last = None
            for k in range(16):
                for tt in range(NTT):
                    last = t.matmul(ps[(p % 2) * 2 + tt][:],
                                    wsl[p % 2][:, k * 128:(k + 1) * 128],
                                    kvbf[:, k, tt * 512:(tt + 1) * 512],
                                    start=(k == 0), stop=(k == 15))
            last.then_inc(tte, 1)
            if p >= 2:
                emit_stats1(p - 2)
        emit_stats1(14)
        emit_stats1(15)
    TT_P3 = TT3 + 32

    @block.vector
    def _(v):
        for p in range(16):
            if p == 0:
                v.wait_ge(s_sp[0], 8 * 16)
                v.wait_ge(s_sp[1], 8 * 16)
            v.wait_ge(tte, att_done(p))
            v.wait_ge(s_xr[p % 2], (p // 2 + 1) * 32)
            for tt in range(NTT):
                v.tensor_mul(hpre[:, p, tt * 512:(tt + 1) * 512], ps[(p % 2) * 2 + tt][:],
                             r42[:, p % 2, tt * 512:(tt + 1) * 512])
            if p >= 2:
                v.wait_ge(sse, SSE3 + (p - 2) + 1)
            v.tensor_add(hpre[:, p, :], hpre[:, p, :], x2[:, p % 2, :]).then_inc(vve, 1)
    VVE_HP = VVE3 + 16

    @block.scalar
    def _(s):
        for p in range(16):
            s.wait_ge(vve, VVE3 + p + 1)
            if p >= 2:
                s.wait_ge(tte, stats1_done(p - 2))
            s.activation(rsig[:, p % 2, :], hpre[:, p, :], AF.Square).then_inc(sse, 1)
    SSE_HSQ = SSE3 + 16

    @block.vector
    def _(v):
        v.wait_ge(tte, TT_P3)
        v.tensor_copy(s_srow[:, 0:512], ps[4][0:1, :])
        v.tensor_copy(s_srow[:, 512:1024], ps[5][0:1, :])
        v.tensor_copy(s_srow[:, 1024:1536], ps[6][0:1, :])
        v.tensor_copy(s_srow[:, 1536:2048], ps[7][0:1, :])
        v.tensor_scalar_mul(s_mu[:], s_srow[:, 0:TOK], 1.0 / D)
        v.tensor_scalar_mul(s_inv[:], s_srow[:, TOK:2 * TOK], 1.0 / D)
        v.tensor_mul(s_srow[:, 0:TOK], s_mu[:], s_mu[:])
        v.tensor_sub(s_inv[:], s_inv[:], s_srow[:, 0:TOK])
        v.tensor_scalar_add(s_inv[:], s_inv[:], EPS).then_inc(vve, 1)
    VVE_VAR1 = VVE_HP + 1

    @block.scalar
    def _(s):
        s.wait_ge(vve, VVE_VAR1)
        s.activation(s_srow[:, 0:TOK], s_inv[:], AF.Sqrt).then_inc(sse, 1)
    SSE_SQ1 = SSE_HSQ + 1

    @block.vector
    def _(v):
        v.wait_ge(sse, SSE_SQ1)
        v.reciprocal(s_inv[:], s_srow[:, 0:TOK]).then_inc(vve, 1)
    VVE_INV1 = VVE_VAR1 + 1

    @block.sync
    def _(sync):
        sync.wait_ge(vve, VVE_INV1)
        sync.dma_start(out=ln1_rows[0:1, :], in_=s_mu[:]).then_inc(s_sq, 16)
        sync.dma_start(out=ln1_rows[1:2, :], in_=s_inv[:]).then_inc(s_sq, 16)
        sync.wait_ge(s_sq, SQ_AG2 + 32)
        sync.dma_start(out=bcst[:, 0, :], in_=ln1_rows[0:1, :].broadcast_to([128, TOK])).then_inc(s_sq, 16)
        sync.dma_start(out=bcst[:, 1, :], in_=ln1_rows[1:2, :].broadcast_to([128, TOK])).then_inc(s_sq, 16)
    SQ_BC1 = SQ_AG2 + 4 * 16


    @block.vector
    def _(v):
        v.wait_ge(s_sq, SQ_BC1)
        for p in range(16):
            v.tensor_sub(hpre[:, p, :], hpre[:, p, :], bcst[:, 0, :])
            v.tensor_mul(hpre[:, p, :], hpre[:, p, :], bcst[:, 1, :])
            v.tensor_copy(hbf[:, p, :], hpre[:, p, :]).then_inc(vve, 1)
    VVE_H1 = VVE_INV1 + 16

    @block.sync
    def _(sync):
        for p in range(16):
            sync.wait_ge(vve, VVE_INV1 + p + 1)
            sync.dma_start(out=hfm_spill[p * 128:(p + 1) * 128, :], in_=hpre[:, p, :]).then_inc(s_sh, 16)
    SH_ALL = 16 * 16

    g_hpre.__exit__(None, None, None)
    g_qb = nc.sbuf_tensor("qb", [64, 2, TOK], BF16); qb = g_qb.__enter__()
    g_q64 = nc.sbuf_tensor("q64", [64, 2 * TOK], F32); q64 = g_q64.__enter__()
    g_q64b = nc.sbuf_tensor("q64b", [64, 2 * TOK], BF16); q64b = g_q64b.__enter__()

    @block.gpsimd
    def _(g):
        g.wait_ge(tte, att_done(15))
        g.dma_start(out=wsl[0][:, 0:16 * 128], in_=wq[0]).then_inc(s_w[0], 16)

    @block.tensor
    def _(t):
        t.wait_ge(s_w[0], 25 * 16)
        t.wait_ge(vve, VVE_H1)
        last = None
        for k in range(16):
            for tt in range(NTT):
                t.matmul(ps[0 + tt][0:64, :], wsl[0][:, k * 128:k * 128 + 64],
                         hbf[:, k, tt * 512:(tt + 1) * 512], start=(k == 0), stop=(k == 15))
                last = t.matmul(ps[2 + tt][0:64, :], wsl[0][:, k * 128 + 64:(k + 1) * 128],
                                hbf[:, k, tt * 512:(tt + 1) * 512], start=(k == 0), stop=(k == 15))
        last.then_inc(tte, 1)
    TT_PROJ = TT_P3 + 1

    @block.vector
    def _(v):
        v.wait_ge(tte, TT_PROJ)
        for tt in range(NTT):
            dst = q64[0:64, tt * 1024:(tt + 1) * 1024].rearrange("p (s two) -> p s two", two=2)
            v.tensor_copy(dst[:, :, 0:1], ps[0 + tt][0:64, :].unsqueeze(2))
            v.tensor_copy(dst[:, :, 1:2], ps[2 + tt][0:64, :].unsqueeze(2))
        v.tensor_copy(q64b[0:64, :], q64[0:64, :]).then_inc(vve, 1)
    VVE_QSL = VVE_H1 + 1

    @block.sync
    def _(sync):
        sync.wait_ge(vve, VVE_QSL)
        sync.dma_start(out=cc3_in, in_=q64b[0:64, :]).then_inc(s_sq, 16)
    SQ_CC3IN = SQ_BC1 + 16

    @block.gpsimd
    def _(g):
        g.wait_ge(s_sq, SQ_CC3IN)
        g.collective_compute("AllGather", ALU.bypass, replica_groups=[list(range(NCORE))],
                             ins=[cc3_in.opt()], outs=[cc3_out.opt()]).then_inc(lcc, 1)
    LCC3 = 3

    g_rsig.__exit__(None, None, None)
    g_q64b.__exit__(None, None, None)
    g_q64.__exit__(None, None, None)
    g_kvbf.__exit__(None, None, None)
    g_mlp = nc.sbuf_tensor("mlpacc", [128, NCH, TOK], F32); mlpacc = g_mlp.__enter__()
    g_sd = nc.sbuf_tensor("sd", [128, TOK], BF16); sd = g_sd.__enter__()
    g_inter = nc.sbuf_tensor("inter", [128, 8, TOK], BF16); inter = g_inter.__enter__()
    g_qg = nc.sbuf_tensor("qg", [64, 4, 2 * TOK], BF16); qg = g_qg.__enter__()
    g_qg.__exit__(None, None, None)
    g_s0t = nc.sbuf_tensor("s0t", [128, 8, TOK], BF16); s0t = g_s0t.__enter__()
    g_s0h2 = nc.sbuf_tensor("s0h2", [128, 2, TOK], BF16); s0h2 = g_s0h2.__enter__()

    TTB, VVEB, SSEB = TT_PROJ, VVE_QSL, SSE_SQ1

    def gu_done(j):
        return TTB + (j // 8) * 24 + (j % 8) + 1

    def dn_done(r, m):
        return TTB + r * 24 + 8 + m + 1

    TT_MLP = TTB + 192

    def rout_done(w):
        return TT_MLP + w + 1

    TT_ROUT = TT_MLP + 8

    Q_ORDER = [("D", 0), ("D", 1), ("T", 0), ("D", 2), ("T", 1), ("D", 3), ("T", 2),
               ("D", 4), ("T", 3), ("D", 5), ("T", 4), ("D", 6), ("T", 5), ("D", 7),
               ("T", 6), ("T", 7)] + [("S", m) for m in range(16)]
    _tmark = {}
    for _q in range(4):
        for _pos, (_k, _i) in enumerate(Q_ORDER):
            _tmark[(_k, _q, _i)] = TT_ROUT + _q * 32 + _pos + 1

    def d_done(g):
        return _tmark[("D", g // 8, g % 8)]

    def t_done(q, w):
        return _tmark[("T", q, w)]

    def sup_done(q, m):
        return _tmark[("S", q, m)]

    TT_P5 = TT_ROUT + 128

    def inv_in(j):
        return VVEB + (j // 8) * 24 + (j % 8) + 1

    def inv_dn(r, m):
        return VVEB + r * 24 + 8 + m + 1

    VVE_MLP = VVEB + 192
    VVE_QA = VVE_MLP + 1
    VVE_QB = VVE_MLP + 2

    def tka(w):
        return VVE_QB + 2 * w + 1

    def tkb(w):
        return VVE_QB + 2 * w + 2

    VVE_TOPK = VVE_QB + 16

    def tev(q, w):
        return VVE_TOPK + q * 32 + w + 1

    def mul_done(g):
        return VVE_TOPK + (g // 8) * 32 + 8 + (g % 8) + 1

    def sup_v(q, m):
        return VVE_TOPK + q * 32 + 16 + m + 1

    VVE_P5 = VVE_TOPK + 128

    def silb(j):
        return SSEB + j + 1

    def exp_s(w):
        return SSEB + 64 + w + 1

    def sil_d(g):
        return SSEB + 72 + g + 1

    SSE_P5 = SSEB + 104

    panel_kinds = []
    panel_tte = []
    for r in range(8):
        for j8 in range(8):
            panel_kinds.append(("gu", r * 8 + j8)); panel_tte.append(gu_done(r * 8 + j8))
        for m in range(16):
            panel_kinds.append(("dn", (r, m))); panel_tte.append(dn_done(r, m))
    for q in range(4):
        for e in range(8):
            panel_kinds.append(("de", q * 8 + e)); panel_tte.append(d_done(q * 8 + e))
        for m in range(16):
            panel_kinds.append(("ue", (q, m))); panel_tte.append(sup_done(q, m))
    N_SYNC_PANELS = 48
    s_wh = [E(nc.semaphore("s_wh0")), E(nc.semaphore("s_wh1"))]
    panel_sem = []
    panel_wv = []
    _wc = [25, 24]
    _wh = [0, 0]
    for _pi, (_kind, _a) in enumerate(panel_kinds):
        _u = 2 if _kind == "gu" else 1
        if _pi < N_SYNC_PANELS:
            _wh[_pi % 2] += _u
            panel_sem.append(s_wh[_pi % 2]); panel_wv.append(_wh[_pi % 2] * 16)
        else:
            _wc[_pi % 2] += _u
            panel_sem.append(s_w[_pi % 2]); panel_wv.append(_wc[_pi % 2] * 16)

    def _emit_panel(eng, pi):
        kind, arg = panel_kinds[pi]
        sem = panel_sem[pi]
        if pi == 0:
            eng.wait_ge(tte, TT_PROJ)
        elif pi == 1:
            eng.wait_ge(tte, att_done(15))
        else:
            eng.wait_ge(tte, panel_tte[pi - 2])
        if kind == "gu":
            eng.dma_start(out=wsl[pi % 2][:, 0:16 * 128], in_=gate_pk[arg]).then_inc(sem, 16)
            eng.dma_start(out=wsl[pi % 2][:, 16 * 128:32 * 128], in_=up_pk[arg]).then_inc(sem, 16)
        elif kind == "dn":
            r, m = arg
            eng.dma_start(out=wsl[pi % 2][:, 0:8 * 128],
                          in_=down_pk[m][:, r * 1024:(r + 1) * 1024]).then_inc(sem, 16)
        elif kind == "de":
            eng.dma_start(out=wsl[pi % 2][:, 0:16 * 128], in_=de_pk[arg]).then_inc(sem, 16)
        else:
            q, m = arg
            eng.dma_start(out=wsl[pi % 2][:, 0:8 * 128],
                          in_=ue_pk[m][:, q * 1024:(q + 1) * 1024]).then_inc(sem, 16)

    @block.sync
    def _(sync):
        for pi in range(N_SYNC_PANELS):
            _emit_panel(sync, pi)
        sync.wait_ge(lcc, LCC3)
        for r in range(4):
            sync.dma_start(out=qg[0:64, r, :], in_=cc3_out[r * 64:(r + 1) * 64, :]).then_inc(s_sq, 16)
        sync.wait_ge(vve, VVE_QA)
        for r in range(4, 8):
            sync.dma_start(out=qg[0:64, r - 4, :], in_=cc3_out[r * 64:(r + 1) * 64, :]).then_inc(s_sq, 16)
    SQ_CCQA = SQ_CC3IN + 4 * 16
    SQ_CCQ = SQ_CC3IN + 8 * 16

    @block.gpsimd
    def _(g):
        for pi in range(N_SYNC_PANELS, len(panel_kinds)):
            _emit_panel(g, pi)
            kind, arg = panel_kinds[pi]
            if kind == "de":
                n = arg
                q, w = divmod(n, 8)
                g.wait_ge(vve, tkb(w))
                if n >= 2:
                    q2, w2 = divmod(n - 2, 8)
                    g.wait_ge(tte, t_done(q2, w2))
                g.local_scatter(s0h2[:, n % 2, :], s_sm[:, w, :], s_iq4[:, w, q, :],
                                channels=128, num_elems=1024, num_idxs=8).then_inc(gge, 1)

    @block.tensor
    def _(t):
        pi = 0
        for r in range(8):
            for j8 in range(8):
                jj = r * 8 + j8
                t.wait_ge(panel_sem[pi], panel_wv[pi]); pi += 1
                if jj < 2:
                    t.wait_ge(vve, VVE_QSL)
                else:
                    t.wait_ge(sse, silb(jj - 2))
                    t.wait_ge(vve, inv_in(jj - 2))
                if r >= 1 and j8 <= 1:
                    t.wait_ge(vve, inv_dn(r - 1, 14 + j8))
                gub = (jj % 2) * 4
                last = None
                for k in range(16):
                    for tt in range(NTT):
                        t.matmul(ps[gub + tt][:], wsl[(pi - 1) % 2][:, k * 128:(k + 1) * 128],
                                 hbf[:, k, tt * 512:(tt + 1) * 512], start=(k == 0), stop=(k == 15))
                        last = t.matmul(ps[gub + 2 + tt][:],
                                        wsl[(pi - 1) % 2][:, 16 * 128 + k * 128:16 * 128 + (k + 1) * 128],
                                        hbf[:, k, tt * 512:(tt + 1) * 512], start=(k == 0), stop=(k == 15))
                last.then_inc(tte, 1)
            for m in range(16):
                t.wait_ge(panel_sem[pi], panel_wv[pi]); pi += 1
                t.wait_ge(vve, inv_in(r * 8 + 7))
                if m >= 2:
                    t.wait_ge(vve, inv_dn(r, m - 2))
                elif r >= 1:
                    t.wait_ge(vve, inv_dn(r - 1, 14 + m))
                last = None
                for k in range(8):
                    for tt in range(NTT):
                        last = t.matmul(ps[4 + (m % 2) * 2 + tt][:],
                                        wsl[(pi - 1) % 2][:, k * 128:(k + 1) * 128],
                                        inter[:, k, tt * 512:(tt + 1) * 512],
                                        start=(k == 0), stop=(k == 7))
                last.then_inc(tte, 1)
        t.wait_ge(vve, VVE_QB)
        for w in range(8):
            if w >= 2:
                t.wait_ge(vve, tka(w - 2))
            else:
                t.wait_ge(vve, inv_dn(7, 14 + w))
            b = 4 + (w % 2) * 2
            t.matmul(ps[b][:, 0:64], qb[0:64, 0, w * 128:(w + 1) * 128],
                     s_keys[0:64, 0, :], start=True, stop=True)
            t.matmul(ps[b + 1][:, 0:64], qb[0:64, 1, w * 128:(w + 1) * 128],
                     s_keys[0:64, 1, :], start=True, stop=True).then_inc(tte, 1)
        for q in range(4):
            for kind, idx in Q_ORDER:
                if kind == "D":
                    gD = q * 8 + idx
                    t.wait_ge(panel_sem[pi], panel_wv[pi]); pi += 1
                    if gD >= 2:
                        t.wait_ge(sse, sil_d(gD - 2))
                    else:
                        t.wait_ge(vve, inv_in(63))
                    pr = (idx % 2) * 2
                    last = None
                    for k in range(16):
                        for tt in range(NTT):
                            last = t.matmul(ps[pr + tt][:],
                                            wsl[(pi - 1) % 2][:, k * 128:(k + 1) * 128],
                                            hbf[:, k, tt * 512:(tt + 1) * 512],
                                            start=(k == 0), stop=(k == 15))
                    last.then_inc(tte, 1)
                elif kind == "T":
                    w = idx
                    n = q * 8 + w
                    t.wait_ge(gge, n + 1)
                    if w < 2:
                        if q == 0:
                            t.wait_ge(vve, tka(6 + w))
                        else:
                            t.wait_ge(vve, sup_v(q - 1, 14 + w))
                    else:
                        t.wait_ge(vve, tev(q, w - 2))
                    b = 4 + (w % 2) * 2
                    lastt = None
                    for e in range(8):
                        lastt = t.matmul(ps[b + e // 4][:, (e % 4) * 128:(e % 4 + 1) * 128],
                                         s0h2[:, n % 2, e * 128:(e + 1) * 128], s_idb[:],
                                         start=True, stop=True)
                    lastt.then_inc(tte, 1)
                else:
                    m = idx
                    t.wait_ge(panel_sem[pi], panel_wv[pi]); pi += 1
                    t.wait_ge(vve, mul_done(q * 8 + 7))
                    if m >= 2:
                        t.wait_ge(vve, sup_v(q, m - 2))
                    pr = 4 + (m % 2) * 2
                    last = None
                    for e in range(8):
                        for tt in range(NTT):
                            last = t.matmul(ps[pr + tt][:],
                                            wsl[(pi - 1) % 2][:, e * 128:(e + 1) * 128],
                                            s0t[:, e, tt * 512:(tt + 1) * 512],
                                            start=(e == 0), stop=(e == 7))
                    last.then_inc(tte, 1)
        assert pi == len(panel_kinds)

    @block.scalar
    def _(s):
        for jj in range(64):
            s.wait_ge(tte, gu_done(jj))
            if jj >= 1:
                s.wait_ge(vve, inv_in(jj - 1))
            gub = (jj % 2) * 4
            s.activation(sd[:, 0:512], ps[gub + 0][:], AF.Silu)
            s.activation(sd[:, 512:1024], ps[gub + 1][:], AF.Silu).then_inc(sse, 1)
        for w in range(8):
            s.wait_ge(vve, tka(w))
            s.activation(s_sm1[:], s_sm1[:], AF.Exp, accum_out=s_ssum[:]).then_inc(sse, 1)
        for g in range(32):
            q, e = divmod(g, 8)
            s.wait_ge(tte, d_done(g))
            if g >= 8:
                s.wait_ge(vve, mul_done(g - 8))
            pr = (e % 2) * 2
            s.activation(inter[:, e, 0:512], ps[pr + 0][:], AF.Silu)
            s.activation(inter[:, e, 512:1024], ps[pr + 1][:], AF.Silu).then_inc(sse, 1)

    @block.vector
    def _(v):
        for r in range(8):
            for j8 in range(8):
                jj = r * 8 + j8
                v.wait_ge(sse, silb(jj))
                gub = (jj % 2) * 4
                for tt in range(NTT):
                    v.tensor_tensor(inter[:, j8, tt * 512:(tt + 1) * 512],
                                    sd[:, tt * 512:(tt + 1) * 512], ps[gub + 2 + tt][:], ALU.mult)
                v.engine_nop().then_inc(vve, 1)
            for m in range(16):
                v.wait_ge(tte, dn_done(r, m))
                if r == 0:
                    if m == 0:
                        v.wait_ge(s_sh, SH_ALL)
                    for tt in range(NTT):
                        v.tensor_copy(mlpacc[:, m, tt * 512:(tt + 1) * 512], ps[4 + (m % 2) * 2 + tt][:])
                else:
                    for tt in range(NTT):
                        v.tensor_add(mlpacc[:, m, tt * 512:(tt + 1) * 512],
                                     mlpacc[:, m, tt * 512:(tt + 1) * 512], ps[4 + (m % 2) * 2 + tt][:])
                v.engine_nop().then_inc(vve, 1)
        for tq in range(2):
            v.wait_ge(s_sq, SQ_CCQA if tq == 0 else SQ_CCQ)
            first = True
            for rr in range(tq * 4, tq * 4 + 4):
                for h in range(2):
                    mcol = s_me3[0:64, tq * 16 + rr * 2 + h: tq * 16 + rr * 2 + h + 1]
                    srcq = qg[0:64, rr - tq * 4, h * 1024:(h + 1) * 1024]
                    if first:
                        v.tensor_scalar(qb[0:64, tq, :], srcq, mcol, None, ALU.mult)
                        first = False
                    else:
                        v.scalar_tensor_tensor(qb[0:64, tq, :], srcq, mcol, qb[0:64, tq, :], ALU.mult, ALU.add)
            v.engine_nop().then_inc(vve, 1)
        v.wait_ge(s_cn, CN_ALL)
        for w in range(8):
            if dev and w == 1:
                v.wait_ge(s_dev, 272 + 16 * 16 + 3 * 16)
            v.wait_ge(tte, rout_done(w))
            b = 4 + (w % 2) * 2
            VB(v.tensor_copy(s_topk[:], ps[b][:, 0:64]), v)
            VB(v.tensor_copy(s_topk2[:], ps[b + 1][:, 0:64]), v)
            VB(v.max(s_t8[:, 0, :], s_topk[:]), v)
            VB(v.max(s_t8[:, 1, :], s_topk2[:]), v)
            for half, srct in ((0, s_topk), (1, s_topk2)):
                VB(v.tensor_tensor(s_eq[:], srct[:].unsqueeze(1).broadcast_to([128, 8, 64]),
                                   s_t8[:, half, :].unsqueeze(2).broadcast_to([128, 8, 64]), ALU.is_equal), v)
                VB(v.tensor_tensor(s_eq[:], s_eq[:], s_iota[:].unsqueeze(1).broadcast_to([128, 8, 64]), ALU.mult), v)
                VB(v.tensor_reduce(s_i8[:, half, :].unsqueeze(2), s_eq[:], AX.X, ALU.add), v)
            VB(v.tensor_tensor(s_a64[:, 0, :].rearrange("p (a b) -> p a b", a=8),
                               s_t8[:, 0, :].unsqueeze(2).broadcast_to([128, 8, 8]),
                               s_t8[:, 1, :].unsqueeze(1).broadcast_to([128, 8, 8]), ALU.add), v)
            VB(v.tensor_scalar_mul(s_i8[:, 3, :], s_i8[:, 0, :], 64.0), v)
            VB(v.tensor_tensor(s_a64[:, 1, :].rearrange("p (a b) -> p a b", a=8),
                               s_i8[:, 3, :].unsqueeze(2).broadcast_to([128, 8, 8]),
                               s_i8[:, 1, :].unsqueeze(1).broadcast_to([128, 8, 8]), ALU.add), v)
            VB(v.max(s_t8[:, 2, :], s_a64[:, 0, :]), v)
            VB(v.tensor_tensor(s_eq[:], s_a64[:, 0, :].unsqueeze(1).broadcast_to([128, 8, 64]),
                               s_t8[:, 2, :].unsqueeze(2).broadcast_to([128, 8, 64]), ALU.is_equal), v)
            VB(v.tensor_tensor(s_eq[:], s_eq[:], s_a64[:, 1, :].unsqueeze(1).broadcast_to([128, 8, 64]), ALU.mult), v)
            VB(v.tensor_reduce(s_i8[:, 2, :].unsqueeze(2), s_eq[:], AX.X, ALU.add), v)
            VB(v.tensor_scalar(s_sm1[:], s_t8[:, 2, :], s_t8[:, 2, 0:1], None, ALU.subtract), v)
            v.engine_nop().then_inc(vve, 1)
            v.wait_ge(sse, exp_s(w))
            VB(v.reciprocal(s_ssum[:], s_ssum[:]), v)
            VB(v.tensor_scalar(s_sm1[:], s_sm1[:], s_ssum[:, 0:1], None, ALU.mult), v)
            VB(v.tensor_copy(s_sm[:, w, :], s_sm1[:]), v)
            for qq in range(4):
                lo = 1024.0 * qq
                VB(v.tensor_scalar(s_t8[:, 3, :], s_i8[:, 2, :], lo, None, ALU.is_ge), v)
                VB(v.tensor_scalar(s_i8[:, 3, :], s_i8[:, 2, :], lo + 1024.0, None, ALU.is_lt), v)
                VB(v.tensor_mul(s_t8[:, 3, :], s_t8[:, 3, :], s_i8[:, 3, :]), v)
                VB(v.tensor_scalar(s_i8[:, 3, :], s_i8[:, 2, :], lo - 1.0, None, ALU.subtract), v)
                VB(v.tensor_mul(s_i8[:, 3, :], s_i8[:, 3, :], s_t8[:, 3, :]), v)
                VB(v.tensor_scalar(s_i8[:, 3, :], s_i8[:, 3, :], -1.0, None, ALU.add), v)
                VB(v.tensor_copy(s_iq4[:, w, qq, :], s_i8[:, 3, :]), v)
            v.engine_nop().then_inc(vve, 1)
        for q in range(4):
            for w in range(8):
                n = q * 8 + w
                v.wait_ge(tte, t_done(q, w))
                b = 4 + (w % 2) * 2
                for e in range(7):
                    v.tensor_copy(s0t[:, e, w * 128:(w + 1) * 128],
                                  ps[b + e // 4][:, (e % 4) * 128:(e % 4 + 1) * 128])
                VB(v.tensor_copy(s0t[:, 7, w * 128:(w + 1) * 128], ps[b + 1][:, 3 * 128:4 * 128]), v)
                v.engine_nop().then_inc(vve, 1)
            for e in range(8):
                v.wait_ge(sse, sil_d(q * 8 + e))
                v.tensor_tensor(s0t[:, e, :], s0t[:, e, :], inter[:, e, :], ALU.mult).then_inc(vve, 1)
            for m in range(16):
                if dev and q == 0 and m == 0:
                    v.wait_ge(s_dev, 272 + 16 * 16)
                v.wait_ge(tte, sup_done(q, m))
                pr = 4 + (m % 2) * 2
                for tt in range(NTT):
                    v.tensor_add(mlpacc[:, m, tt * 512:(tt + 1) * 512],
                                 mlpacc[:, m, tt * 512:(tt + 1) * 512], ps[pr + tt][:])
                v.engine_nop().then_inc(vve, 1)

    if dev:
        @block.sync
        def _(sync):
            for m in range(16):
                sync.wait_ge(vve, inv_dn(7, m))
                sync.dma_start(out=macc_dump[m * 128:(m + 1) * 128, :],
                               in_=mlpacc[:, m, :]).then_inc(s_dev, 16)
            sync.wait_ge(vve, tkb(0))
            sync.dma_start(out=tk_dump[:, 0:64], in_=s_topk[:]).then_inc(s_dev, 16)
            sync.dma_start(out=tk_dump[:, 64:128], in_=s_topk2[:]).then_inc(s_dev, 16)
            sync.dma_start(out=tk_dump[:, 128:160], in_=s_t8[:].rearrange("p a b -> p (a b)")).then_inc(s_dev, 16)


    g_s0h2.__exit__(None, None, None)
    g_s0t.__exit__(None, None, None)
    g_inter.__exit__(None, None, None)
    g_hsq2 = nc.sbuf_tensor("hsq2", [128, 2, TOK], F32); hsq2 = g_hsq2.__enter__()
    TT6, VVE6, SSE6 = TT_P5, VVE_P5, SSE_P5

    @block.sync
    def _(sync):
        sync.wait_ge(s_sh, SH_ALL)
        for p in range(16):
            if p >= 2:
                sync.wait_ge(vve, VVE6 + (p - 2) + 1)
            sync.dma_start(out=x2[:, p % 2, :], in_=hfm_spill[p * 128:(p + 1) * 128, :]).then_inc(s_xr[p % 2], 16)

    @block.vector
    def _(v):
        for p in range(16):
            v.wait_ge(s_xr[p % 2], 8 * 32 + (p // 2 + 1) * 16)
            v.wait_ge(vve, sup_v(3, p))
            v.tensor_add(mlpacc[:, p, :], mlpacc[:, p, :], x2[:, p % 2, :]).then_inc(vve, 1)
    VVE_RES2 = VVE6 + 16

    @block.scalar
    def _(s):
        for p in range(16):
            s.wait_ge(vve, VVE6 + p + 1)
            if p >= 2:
                s.wait_ge(tte, TT6 + (p - 2) + 1)
            s.activation(hsq2[:, p % 2, :], mlpacc[:, p, :], AF.Square).then_inc(sse, 1)
    SSE_SQ2 = SSE6 + 16

    @block.tensor
    def _(t):
        t.wait_ge(vve, VVE_P5)
        for p in range(16):
            t.wait_ge(sse, SSE6 + p + 1)
            last = None
            for tt in range(NTT):
                t.matmul(ps[4 + tt][0:1, :], s_ones[:], mlpacc[:, p, tt * 512:(tt + 1) * 512],
                         start=(p == 0), stop=(p == 15))
                last = t.matmul(ps[6 + tt][0:1, :], s_ones[:], hsq2[:, p % 2, tt * 512:(tt + 1) * 512],
                                start=(p == 0), stop=(p == 15))
            last.then_inc(tte, 1)
    TT_ST2 = TT6 + 16

    @block.vector
    def _(v):
        v.wait_ge(tte, TT_ST2)
        v.tensor_copy(s_srow[:, 0:512], ps[4][0:1, :])
        v.tensor_copy(s_srow[:, 512:1024], ps[5][0:1, :])
        v.tensor_copy(s_srow[:, 1024:1536], ps[6][0:1, :])
        v.tensor_copy(s_srow[:, 1536:2048], ps[7][0:1, :])
        v.tensor_scalar_mul(s_mu[:], s_srow[:, 0:TOK], 1.0 / D)
        v.tensor_scalar_mul(s_inv[:], s_srow[:, TOK:2 * TOK], 1.0 / D)
        v.tensor_mul(s_srow[:, 0:TOK], s_mu[:], s_mu[:])
        v.tensor_sub(s_inv[:], s_inv[:], s_srow[:, 0:TOK])
        v.tensor_scalar_add(s_inv[:], s_inv[:], EPS).then_inc(vve, 1)
    VVE_VAR2 = VVE_RES2 + 1

    @block.scalar
    def _(s):
        s.wait_ge(vve, VVE_VAR2)
        s.activation(s_srow[:, 0:TOK], s_inv[:], AF.Sqrt).then_inc(sse, 1)
    SSE_SQRT2 = SSE_SQ2 + 1

    @block.vector
    def _(v):
        v.wait_ge(sse, SSE_SQRT2)
        v.reciprocal(s_inv[:], s_srow[:, 0:TOK]).then_inc(vve, 1)
    VVE_INV2 = VVE_VAR2 + 1

    @block.sync
    def _(sync):
        sync.wait_ge(vve, VVE_INV2)
        sync.dma_start(out=ln2_rows[0:1, :], in_=s_mu[:]).then_inc(s_sq, 16)
        sync.dma_start(out=ln2_rows[1:2, :], in_=s_inv[:]).then_inc(s_sq, 16)
        sync.wait_ge(s_sq, SQ_CCQ + 32)
        sync.dma_start(out=bcst[:, 0, :], in_=ln2_rows[0:1, :].broadcast_to([128, TOK])).then_inc(s_sq, 16)
        sync.dma_start(out=bcst[:, 1, :], in_=ln2_rows[1:2, :].broadcast_to([128, TOK])).then_inc(s_sq, 16)
    SQ_BC2 = SQ_CCQ + 4 * 16

    @block.vector
    def _(v):
        v.wait_ge(s_sq, SQ_BC2)
        for p in range(16):
            v.tensor_sub(mlpacc[:, p, :], mlpacc[:, p, :], bcst[:, 0, :])
            v.tensor_mul(mlpacc[:, p, :], mlpacc[:, p, :], bcst[:, 1, :]).then_inc(vve, 1)
    VVE_OUT = VVE_INV2 + 16

    @block.sync
    def _(sync):
        for p in range(16):
            sync.wait_ge(vve, VVE_INV2 + p + 1)
            sync.dma_start(out=out_o[p * 128:(p + 1) * 128, :], in_=mlpacc[:, p, :]).then_inc(s_out, 16)
        sync.wait_ge(s_out, 16 * 16)

    return nc


def _packT(w, s=None):
    co, ci = w.shape
    nj, nk = co // 128, ci // 128
    a = w.reshape(nj, 128, nk, 128)
    a = a.transpose(0, 3, 2, 1)
    return np.ascontiguousarray(a.reshape(nj, 128, nk * 128))


def _pack_nat(w):
    ci, co = w.shape
    nk, nj = ci // 128, co // 128
    a = w.reshape(nk, 128, nj, 128)
    a = a.transpose(2, 1, 0, 3)
    return np.ascontiguousarray(a.reshape(nj, 128, nk * 128))


_CACHE = {}


def kernel(L0_attn_x, L0_attn_W_act, L0_attn_W_val, L0_attn_W_rec, L0_ffn_W_q,
           L0_ffn_keys, L0_ffn_down_embed, L0_ffn_up_embed, L0_ffn_gate_w,
           L0_ffn_up_w, L0_ffn_down_w):
    x = np.asarray(L0_attn_x, dtype=np.float32)
    W_act = np.asarray(L0_attn_W_act, np.float32)
    W_val = np.asarray(L0_attn_W_val, np.float32)
    W_rec = np.asarray(L0_attn_W_rec, np.float32)
    W_q = np.asarray(L0_ffn_W_q, np.float32)
    keys = np.asarray(L0_ffn_keys, np.float32)
    de = np.asarray(L0_ffn_down_embed, np.float32)
    ue = np.asarray(L0_ffn_up_embed, np.float32)
    gw = np.asarray(L0_ffn_gate_w, np.float32)
    uw = np.asarray(L0_ffn_up_w, np.float32)
    dw = np.asarray(L0_ffn_down_w, np.float32)

    perm = np.concatenate([np.arange(0, D, 2), np.arange(1, D, 2)])
    shared = dict(
        wact_re=_packT(W_act[0::2], None),
        wact_im=_packT(W_act[1::2], None),
        wrec=_packT(W_rec),
        wval=_packT(np.ascontiguousarray(W_val[:, perm])),
        wq=_pack_nat(W_q),
        keys_in=np.ascontiguousarray(keys),
        de_pk=_packT(de).astype(ml_dtypes.bfloat16),
        ue_pk=_pack_nat(ue).astype(ml_dtypes.bfloat16),
        gate_pk=_pack_nat(gw).astype(ml_dtypes.bfloat16),
        up_pk=_pack_nat(uw).astype(ml_dtypes.bfloat16),
        down_pk=_pack_nat(dw).astype(ml_dtypes.bfloat16),
        identb=np.eye(128, dtype=ml_dtypes.bfloat16),
        onesb=np.ones((128, 1), dtype=np.float32),
        iota64=np.broadcast_to(np.arange(64, dtype=np.float32), (128, 64)).copy(),
    )

    in_maps = []
    for c in range(NCORE):
        b, q = c // 4, c % 4
        s0 = TOK * q
        xs = x[b, s0:s0 + TOK, :]
        halo = x[b, s0 - 1, :] if q > 0 else np.zeros(D, np.float32)
        xf = np.empty((NCH, 128, TOK + 1), np.float32)
        xt = np.ascontiguousarray(xs.T)
        for k in range(NCH):
            xf[k, :, 0] = halo[k * 128:(k + 1) * 128]
            xf[k, :, 1:] = xt[k * 128:(k + 1) * 128, :]
        mm = np.full((128, 8), -1e30, np.float32)
        ma = np.zeros((128, 8), np.float32)
        for r in range(4 * b, c):
            mm[:, r] = 0.0
            ma[:, r] = 1.0
        me3 = np.zeros((64, 32), np.float32)
        me3[:, 0 * 16 + (c // 2) * 2 + (c % 2)] = 1.0
        me3[:, 1 * 16 + (4 + c // 2) * 2 + (c % 2)] = 1.0
        im = dict(shared)
        im.update(xfm=xf, mask_max=mm, mask_add=ma, mask_e3=me3)
        in_maps.append(im)

    nc = build_nc()
    lower_extended_insts(nc)
    res = run_bass_kernel_spmd(nc, in_maps, core_ids=list(range(NCORE)))

    out = np.empty((2, 4096, D), np.float32)
    for c in range(NCORE):
        b, q = c // 4, c % 4
        out[b, TOK * q:TOK * (q + 1), :] = np.asarray(res.results[c]["o"]).T
    return out
